# version 08
# baseline (speedup 1.0000x reference)
import numpy as np
import ml_dtypes
from contextlib import ExitStack

import concourse.bass as bass
import concourse.mybir as mybir
from concourse.bass_utils import run_bass_kernel_spmd

F32 = mybir.dt.float32
BF16 = mybir.dt.bfloat16
AF = mybir.ActivationFunctionType
ALU = mybir.AluOpType
AX = mybir.AxisListType

NCORE = 8
B_PER = 2
S = 2048
D = 1024
NH = 16
DH = 64
NE = 16
D_INNER = 2048
D_STATE = 128
NHEADS = 32
HEADDIM = 64
CONV_DIM = 2304
D_IN_PROJ = 4384
HIDDEN = 4096
Q = 128
NCHUNK = S // Q


COMPUTE_ENGINES = {"pe": "tensor", "act": "scalar", "dve": "vector", "pool": "gpsimd"}
DMA_QUEUES = {"qs": "sync", "qp": "gpsimd", "qa": "scalar"}


DMA_SLOTS = {"qs": 10, "qp": 8, "qa": 6}


class Prog:
    def __init__(self, nc, stack):
        self.nc = nc
        self.stack = stack
        self.sems = {}
        self.cnt = {}
        for k in COMPUTE_ENGINES:
            self.sems[k] = stack.enter_context(nc.semaphore(f"sem_{k}"))
            self.cnt[k] = 0
        for q, n in DMA_SLOTS.items():
            for i in range(n):
                sk = f"{q}{i}"
                self.sems[sk] = stack.enter_context(nc.semaphore(f"sem_{sk}"))
                self.cnt[sk] = 0
        self.rr = {q: 0 for q in DMA_SLOTS}
        waiters = list(COMPUTE_ENGINES) + list(DMA_QUEUES)
        self.wm = {k: {} for k in waiters}
        self.writer = {}
        self.readers = {}
        self.sbufs = {}
        self.psums = {}
        self.sbuf_bytes = 0

    def sbuf(self, name, shape, dtype):
        t = self.sbufs.get(name)
        if t is None:
            t = self.nc.alloc_sbuf_tensor(name, shape, dtype)
            self.sbufs[name] = t
            self.sbuf_bytes += shape[1] * mybir.dt.size(dtype)
        return t

    def psum(self, name, shape, dtype=F32):
        t = self.psums.get(name)
        if t is None:
            t = self.nc.alloc_psum_tensor(name, shape, dtype)
            self.psums[name] = t
        return t

    @staticmethod
    def _ap(x):
        return x if isinstance(x, bass.AP) else x[:, :]

    def _keys(self, ap):
        h = ap.tensor
        name = getattr(h, "name", None) or str(id(h))
        if h in getattr(self.nc, "_psum_set", ()):
            pass
        bucket = None
        if name in self.psums:
            bucket = 2048
        elif name in ("A0", "A1", "AF0", "W16"):
            bucket = 256
        if bucket is not None:
            dsz = mybir.dt.size(ap.dtype)
            pstep = ap.ap[0][0] if ap.ap else 1
            col = (ap.offset % pstep) if pstep > 0 else ap.offset
            off = col * dsz
            span = 1
            for step, count in ap.ap[1:]:
                span += (count - 1) * abs(step)
            span *= dsz
            b0 = off // bucket
            b1 = (off + span - 1) // bucket
            return [("ps:" + name, b) for b in range(b0, b1 + 1)]
        return [("t:" + name, 0)]

    def _require(self, waiter, deps):
        for dk, dc in deps.items():
            if dc <= 0:
                continue
            if waiter == "pe" and dk == "pe":
                continue
            if self.wm[waiter].get(dk, 0) >= dc:
                continue
            getattr(self.nc, COMPUTE_ENGINES.get(waiter) or DMA_QUEUES[waiter]).wait_ge(
                self.sems[dk], dc)
            self.wm[waiter][dk] = dc

    def _collect(self, ins, outs):
        deps = {}
        for ap in ins:
            for k in self._keys(ap):
                w = self.writer.get(k)
                if w:
                    deps[w[0]] = max(deps.get(w[0], 0), w[1])
        for ap in outs:
            for k in self._keys(ap):
                w = self.writer.get(k)
                if w:
                    deps[w[0]] = max(deps.get(w[0], 0), w[1])
                for rk, rc in self.readers.get(k, {}).items():
                    deps[rk] = max(deps.get(rk, 0), rc)
        return deps

    def _commit(self, semkey, inc, ins, outs):
        c = self.cnt[semkey] + inc
        self.cnt[semkey] = c
        for ap in ins:
            for k in self._keys(ap):
                self.readers.setdefault(k, {})[semkey] = c
        for ap in outs:
            for k in self._keys(ap):
                self.writer[k] = (semkey, c)
                self.readers[k] = {}

    def _do(self, semkey, inc, ins, outs, emit):
        ins = [self._ap(x) for x in ins]
        outs = [self._ap(x) for x in outs]
        self._require(semkey, self._collect(ins, outs))
        instr = emit()
        instr.then_inc(self.sems[semkey], inc)
        self._commit(semkey, inc, ins, outs)
        return instr

    def dma(self, q, out, in_):
        eng = getattr(self.nc, DMA_QUEUES[q])
        out = self._ap(out)
        in_ = self._ap(in_)
        slot = self.rr[q] % DMA_SLOTS[q]
        self.rr[q] += 1
        sk = f"{q}{slot}"
        deps = self._collect([in_], [out])
        deps[sk] = max(deps.get(sk, 0), self.cnt[sk])
        self._require(q, deps)
        eng.dma_start(out, in_).then_inc(self.sems[sk], 16)
        self._commit(sk, 16, [in_], [out])

    def mm(self, out, lhsT, rhs, start=True, stop=True, tile_position=None):
        e = self.nc.tensor
        return self._do("pe", 1, [lhsT, rhs], [out],
                        lambda: e.matmul(self._ap(out), self._ap(lhsT), self._ap(rhs),
                                         start=start, stop=stop, tile_position=tile_position))

    def mm_group(self, out, pairs):
        e = self.nc.tensor
        ins = []
        for l, r in pairs:
            ins += [self._ap(l), self._ap(r)]
        outs = [self._ap(out)]
        self._require("pe", self._collect(ins, outs))
        n = len(pairs)
        for i, (l, r) in enumerate(pairs):
            instr = e.matmul(outs[0], self._ap(l), self._ap(r),
                             start=(i == 0), stop=(i == n - 1))
        instr.then_inc(self.sems["pe"], 1)
        self._commit("pe", 1, ins, outs)

    def transpose(self, out, in_, ident):
        e = self.nc.tensor
        return self._do("pe", 1, [in_, ident], [out],
                        lambda: e.transpose(self._ap(out), self._ap(in_), self._ap(ident)))

    def act(self, out, in_, func, bias=0.0, scale=1.0, accum_out=None):
        e = self.nc.scalar
        if not isinstance(bias, (int, float)):
            bias = self._ap(bias)
        if not isinstance(scale, (int, float)):
            scale = self._ap(scale)
        ins = [in_] + [x for x in (bias, scale) if isinstance(x, bass.AP)]
        outs = [out] + ([accum_out] if accum_out is not None else [])
        kw = dict(bias=bias, scale=scale)
        if accum_out is not None:
            kw["accum_out"] = self._ap(accum_out)
        return self._do("act", 1, ins, outs,
                        lambda: e.activation(self._ap(out), self._ap(in_), func, **kw))

    def _viface(self, engkey):
        return self.nc.vector if engkey == "dve" else self.nc.gpsimd

    def copy(self, engkey, out, in_):
        e = self._viface(engkey)
        return self._do(engkey, 1, [in_], [out], lambda: e.tensor_copy(self._ap(out), self._ap(in_)))

    def tt(self, engkey, out, in0, in1, op):
        e = self._viface(engkey)
        return self._do(engkey, 1, [in0, in1], [out],
                        lambda: e.tensor_tensor(self._ap(out), self._ap(in0), self._ap(in1), op=op))

    def ts(self, engkey, out, in0, s1, op0, s2=None, op1=None):
        e = self._viface(engkey)
        ins = [in0] + [x for x in (s1, s2) if isinstance(x, bass.AP)]
        kw = {}
        if op1 is not None:
            kw["op1"] = op1
        return self._do(engkey, 1, ins, [out],
                        lambda: e.tensor_scalar(self._ap(out), self._ap(in0),
                                                self._ap(s1) if isinstance(s1, bass.AP) else s1,
                                                self._ap(s2) if isinstance(s2, bass.AP) else s2,
                                                op0, **kw))

    def stt(self, engkey, out, in0, scalar, in1, op0, op1):
        e = self._viface(engkey)
        ins = [in0, in1] + ([scalar] if isinstance(scalar, bass.AP) else [])
        return self._do(engkey, 1, ins, [out],
                        lambda: e.scalar_tensor_tensor(
                            self._ap(out), self._ap(in0),
                            self._ap(scalar) if isinstance(scalar, bass.AP) else scalar,
                            self._ap(in1), op0=op0, op1=op1))

    def red(self, engkey, out, in_, op, axis=None, negate=False):
        e = self._viface(engkey)
        axis = axis if axis is not None else AX.X
        return self._do(engkey, 1, [in_], [out],
                        lambda: e.tensor_reduce(self._ap(out), self._ap(in_), axis=axis,
                                                op=op, negate=negate))

    def recip(self, out, in_):
        e = self.nc.vector
        return self._do("dve", 1, [in_], [out], lambda: e.reciprocal(self._ap(out), self._ap(in_)))

    def scan(self, out, d0, d1, initial, op0, op1):
        e = self.nc.vector
        return self._do("dve", 1, [d0, d1], [out],
                        lambda: e.tensor_tensor_scan(self._ap(out), self._ap(d0), self._ap(d1),
                                                     initial, op0=op0, op1=op1))

    def memset(self, out, val):
        e = self.nc.gpsimd
        return self._do("pool", 1, [], [out], lambda: e.memset(self._ap(out), val))

    def affine_sel(self, out, in_, compare_op, fill, base, pattern, channel_multiplier):
        e = self.nc.gpsimd
        return self._do("pool", 1, [in_], [out],
                        lambda: e.affine_select(out=self._ap(out), in_=self._ap(in_),
                                                compare_op=compare_op, fill=fill, base=base,
                                                pattern=pattern,
                                                channel_multiplier=channel_multiplier))

    def finish(self):
        for k, c in self.cnt.items():
            if c > 0:
                self.nc.sync.wait_ge(self.sems[k], c)

    def const(self, T, name):
        shape, dt = WEIGHT_SPECS[name]
        t = self.sbufs.get("K_" + name)
        if t is None:
            t = self.sbuf("K_" + name, shape, dt)
            self.dma("qs", t, T[name])
        return t

    def bank(self, i):
        t = self.psum("PSE", [128, 3072])
        return t[:, i * 512:(i + 1) * 512]

    def fbias(self, val, parts=128):
        key = f"fb_{val}"
        t = self.sbufs.get(key)
        if t is None:
            t = self.sbuf(key, [128, 1], F32)
            self.memset(t, val)
        return t[0:parts, 0:1]



def prep_inputs(inputs):
    f = np.ascontiguousarray
    d = {}
    d["wqT"] = f(inputs["L0_attn_W_Q"].transpose(2, 0, 1).reshape(1024, 1024))
    d["wkT"] = f(inputs["L0_attn_W_K"].transpose(2, 0, 1).reshape(1024, 1024))
    d["wvT"] = f(inputs["L0_attn_W_V"].transpose(2, 0, 1).reshape(1024, 1024))
    d["woT"] = f(inputs["L0_attn_W_O"].T)
    d["sim"] = f(inputs["L0_ffn_sim_matrix"])
    d["gates"] = f(inputs["L0_ffn_gates"].reshape(1, 16))
    qkv = np.concatenate([inputs["L0_ffn_q_proj"], inputs["L0_ffn_k_proj"],
                          inputs["L0_ffn_v_proj"]], axis=2)
    d["qkv_moe"] = f(qkv.transpose(1, 0, 2).reshape(1024, 16 * 192))
    d["o_stack"] = f(inputs["L0_ffn_o_proj"].reshape(1024, 1024))
    d["inprojT"] = f(inputs["L1_attn_in_proj_w"].T)
    d["conv_w"] = f(inputs["L1_attn_conv_w"])
    d["conv_b"] = f(inputs["L1_attn_conv_b"].reshape(2304, 1))
    d["dt_bias"] = f(inputs["L1_attn_dt_bias"].reshape(32, 1))
    d["A_log"] = f(inputs["L1_attn_A_log"].reshape(32, 1))
    d["D_rep"] = f(np.repeat(inputs["L1_attn_D"], 64).reshape(1, 2048))
    d["norm_w"] = f(inputs["L1_attn_norm_w"].reshape(1, 2048))
    d["outprojT"] = f(inputs["L1_attn_out_proj_w"].T)
    d["gateT"] = f(inputs["L1_attn_gate_w"].T)
    d["upT"] = f(inputs["L1_attn_up_w"].T)
    d["downT"] = f(inputs["L1_attn_down_w"].T)
    d["mixer_scale"] = f(inputs["L1_attn_mixer_scale"].reshape(1, 1024))
    d["mlp_scale"] = f(inputs["L1_attn_mlp_scale"].reshape(1, 1024))
    d["l1fwT"] = f(inputs["L1_ffn_weight"].T)
    d["disha"] = f(inputs["L1_ffn_disha"])
    d["c_ident"] = np.eye(128, dtype=np.float32)
    d["c_identb"] = np.eye(128).astype(ml_dtypes.bfloat16)
    d["c_ones"] = np.ones((128, 128), dtype=np.float32)
    d["c_onesb"] = np.ones((128, 128), dtype=ml_dtypes.bfloat16)
    tril = (np.arange(128)[:, None] <= np.arange(128)[None, :]).astype(np.float32)
    d["c_trilU"] = tril
    sel = np.zeros((128, 64), dtype=np.float32)
    sel[np.arange(128), np.arange(128) % 64] = 1.0
    d["c_selpool"] = sel.astype(ml_dtypes.bfloat16)
    mt = np.where(np.arange(128)[None, :] > np.arange(128)[:, None], -1e9, 0.0)
    d["c_maskT"] = mt.astype(np.float32)
    e127 = np.zeros((128, 1), dtype=np.float32)
    e127[127, 0] = 1.0
    d["c_e127"] = e127
    ii = np.zeros((64, 128), dtype=np.float32)
    ii[np.arange(128) % 64, np.arange(128)] = 1.0
    d["c_II"] = ii.astype(ml_dtypes.bfloat16)
    oh2 = np.zeros((16, 8, 128), dtype=np.float32)
    for pr in range(8):
        oh2[2 * pr, pr, 0:64] = 1.0
        oh2[2 * pr + 1, pr, 64:128] = 1.0
    d["c_oh2"] = oh2.reshape(16, 1024).astype(ml_dtypes.bfloat16)
    return d


WEIGHT_SPECS = {
    "wqT": ([1024, 1024], F32), "wkT": ([1024, 1024], F32), "wvT": ([1024, 1024], F32),
    "woT": ([1024, 1024], F32), "sim": ([1024, 16], F32), "gates": ([1, 16], F32),
    "qkv_moe": ([1024, 3072], F32), "o_stack": ([1024, 1024], F32),
    "inprojT": ([1024, 4384], F32), "conv_w": ([2304, 4], F32), "conv_b": ([2304, 1], F32),
    "dt_bias": ([32, 1], F32), "A_log": ([32, 1], F32), "D_rep": ([1, 2048], F32),
    "norm_w": ([1, 2048], F32), "outprojT": ([2048, 1024], F32),
    "gateT": ([1024, 4096], F32), "upT": ([1024, 4096], F32), "downT": ([4096, 1024], F32),
    "mixer_scale": ([1, 1024], F32), "mlp_scale": ([1, 1024], F32),
    "l1fwT": ([1024, 1024], F32), "disha": ([64, 1024], F32),
    "c_ident": ([128, 128], F32), "c_identb": ([128, 128], BF16),
    "c_ones": ([128, 128], F32), "c_onesb": ([128, 128], BF16),
    "c_trilU": ([128, 128], F32), "c_selpool": ([128, 64], BF16),
    "c_maskT": ([128, 128], F32), "c_II": ([64, 128], BF16), "c_oh2": ([16, 1024], BF16),
    "c_e127": ([128, 1], F32),
}

SCRATCH_SPECS = {
    "c_xnT": ([B_PER, 8, 128, S], BF16),
    "c_rw": ([B_PER, 16, 128, 16], F32),
    "c_rwT": ([B_PER, 16, S], BF16),
    "c_zflatT": ([B_PER, 8, 128, S], BF16),
    "c_amT": ([B_PER, 64, S], BF16),
    "c_s0T": ([B_PER, 8, 128, S], F32),
    "c_x1": ([B_PER, S, D], F32),
    "c_xn1": ([B_PER, S, D], F32),
    "c_xn1T": ([B_PER, 8, 128, S], BF16),
    "c_uT": ([B_PER, 8, 128, S], BF16),
    "c_zT": ([B_PER, 16, 128, S], BF16),
    "c_xbcT": ([B_PER, 18, 128, S], BF16),
    "c_dt": ([B_PER, 32, S], F32),
    "c_dtA": ([B_PER, 32, S], F32),
    "c_xp": ([B_PER, S, D], F32),
    "c_h2T": ([B_PER, 8, 128, S], BF16),
    "c_midT": ([32, 128, B_PER * S], BF16),
    "c_s1": ([B_PER, S, D], F32),
    "c_g1T": ([B_PER, 8, 128, S], F32),
}

PHASE_WRITES = {
    "p0": ["c_xnT", "c_rw", "c_rwT"],
    "p1": ["c_zflatT"],
    "p1b": ["c_s0T"],
    "p2": ["c_amT"],
    "p2b": ["c_x1"],
    "p3": ["c_xn1", "c_xn1T", "c_uT"],
    "p4": ["c_zT", "c_xbcT", "c_dt", "c_dtA"],
    "p5a": ["c_xp", "c_h2T"],
    "p5c": ["c_midT", "c_s1"],
    "p5d": ["c_g1T"],
    "p5e": [],
}
PHASE_READS = {
    "p0": [],
    "p1": ["c_xnT"],
    "p1b": ["c_zflatT"],
    "p2": ["c_xnT", "c_rw"],
    "p2b": ["c_rwT", "c_s0T", "c_amT"],
    "p3": ["c_x1"],
    "p4": ["c_uT"],
    "p5a": ["c_zT", "c_xbcT", "c_dt", "c_dtA", "c_xn1"],
    "p5c": ["c_h2T", "c_xp", "c_midT"],
    "p5d": ["c_xn1T"],
    "p5e": ["c_x1", "c_s1", "c_g1T"],
}
PHASE_NEEDS_X = {"p0", "p2b"}
PHASE_WRITES_OUT = {"p5e"}

ALL_PHASES = ["p0", "p1", "p1b", "p2", "p2b", "p3", "p4", "p5a", "p5c", "p5d", "p5e"]
GROUPS = [ALL_PHASES]



def stream_weight_b16(p, T, name, arena, off, n_in, n_out):
    nch = n_in // 128
    stg = p.sbuf("wstg", [128, 2 * 512], F32)
    src = T[name]
    i = 0
    for k in range(nch):
        for c0 in range(0, n_out, 512):
            cw = min(512, n_out - c0)
            half = (i % 2) * 512
            i += 1
            p.dma("qs", stg[:, half:half + cw], src[k * 128:(k + 1) * 128, c0:c0 + cw])
            dst = arena[:, off + k * n_out + c0: off + k * n_out + c0 + cw]
            if i % 2:
                p.copy("dve", dst, stg[:, half:half + cw])
            else:
                p.act(dst, stg[:, half:half + cw], AF.Identity)

    def chunk(k, c0, c1):
        return arena[:, off + k * n_out + c0: off + k * n_out + c1]
    return chunk



def emit_p0(p, T):
    ident = p.const(T, "c_ident")
    identb = p.const(T, "c_identb")
    onesf = p.const(T, "c_ones")

    simc = p.sbuf("simc", [128, 8 * 16], F32)
    for k in range(8):
        p.dma("qs", simc[:, k * 16:(k + 1) * 16], T["sim"][k * 128:(k + 1) * 128, :])
    ssq = p.sbuf("p0_ssq", [128, 16], F32)
    psg = p.psum("PSC", [128, 512], F32)
    pairs = []
    for k in range(8):
        p.act(ssq, simc[:, k * 16:(k + 1) * 16], AF.Square)
        p.mm(psg[0:1, 0:16], onesf[:, 0:1], ssq, start=(k == 0), stop=(k == 7))
    nsr = p.sbuf("p0_nsr", [1, 64], F32)
    p.act(nsr[0:1, 0:16], psg[0:1, 0:16], AF.Sqrt)
    p.ts("dve", nsr[0:1, 16:32], nsr[0:1, 0:16], 1e-12, ALU.max)
    p.recip(nsr[0:1, 32:48], nsr[0:1, 16:32])
    p.mm(psg[:, 16:32], onesf[0:1, :], nsr[0:1, 32:48])
    rnsB = p.sbuf("p0_rnsB", [128, 16], F32)
    p.act(rnsB, psg[:, 16:32], AF.Identity)
    simh = p.sbuf("p0_simh", [128, 8 * 16], F32)
    for k in range(8):
        p.tt("dve", simh[:, k * 16:(k + 1) * 16], simc[:, k * 16:(k + 1) * 16], rnsB, ALU.mult)
    grow = p.sbuf("p0_grow", [1, 16], F32)
    p.dma("qs", grow, T["gates"])
    gsig = p.sbuf("p0_gsig", [1, 16], F32)
    p.act(gsig, grow, AF.Sigmoid)
    p.mm(psg[:, 32:48], onesf[0:1, :], gsig)
    gB = p.sbuf("p0_gB", [128, 16], F32)
    p.act(gB, psg[:, 32:48], AF.Identity)

    xt = [p.sbuf(f"p0_x{i}", [128, 1024], F32) for i in range(2)]
    xaT = [p.sbuf(f"p0_xaT{i}", [128, 128], F32) for i in range(2)]
    xnTb = [p.sbuf(f"p0_xnTb{i}", [128, 128], BF16) for i in range(2)]
    psT = [p.bank(0), p.bank(1)]
    psL = p.bank(2)
    col = p.sbuf("p0_col", [128, 16], F32)
    xc = p.sbuf("p0_xc", [128, 1024], F32)
    sq = p.sbuf("p0_sq", [128, 1024], F32)
    xn = p.sbuf("p0_xn", [128, 1024], F32)
    ga = p.sbuf("p0_ga", [128, 16 * 8], F32)
    rwb = p.sbuf("p0_rwb", [128, 16], BF16)
    rwTb = p.sbuf("p0_rwTb", [16, 128], BF16)
    psB = p.psum("PSB", [128, 1024], BF16)

    for s in range(B_PER):
        for t in range(16):
            xb = xt[t % 2]
            p.dma("qs", xb, T["x"][s, t * 128:(t + 1) * 128, :])
            p.red("dve", col[:, 0:1], xb, ALU.add)
            p.ts("dve", col[:, 1:2], col[:, 0:1], 1.0 / 1024, ALU.mult)
            p.ts("dve", xc, xb, col[:, 1:2], ALU.subtract)
            p.act(sq, xc, AF.Square)
            p.red("dve", col[:, 2:3], sq, ALU.add)
            p.act(col[:, 3:4], col[:, 2:3], AF.Sqrt, scale=1.0 / 1024, bias=p.fbias(1e-5))
            p.recip(col[:, 4:5], col[:, 3:4])
            p.ts("dve", xn, xc, col[:, 4:5], ALU.mult)
            p.act(sq, xn, AF.Square)
            p.red("dve", col[:, 5:6], sq, ALU.add)
            p.act(col[:, 6:7], col[:, 5:6], AF.Sqrt)
            p.ts("dve", col[:, 7:8], col[:, 6:7], 1e-12, ALU.max)
            p.recip(col[:, 8:9], col[:, 7:8])
            for k in range(8):
                ps = psT[k % 2]
                p.transpose(ps[:, 0:128], xn[:, k * 128:(k + 1) * 128], ident)
                xa = xaT[k % 2]
                p.act(xa, ps[:, 0:128], AF.Identity)
                p.act(xnTb[k % 2], ps[:, 0:128], AF.Identity)
                p.dma("qp", T["c_xnT"][s, k, :, t * 128:(t + 1) * 128], xnTb[k % 2])
                p.mm(psL[:, 0:16], xa, simh[:, k * 16:(k + 1) * 16],
                     start=(k == 0), stop=(k == 7))
            lg = ga[:, 0:16]; pos = ga[:, 16:32]; gated = ga[:, 32:48]
            l2 = ga[:, 48:64]; fb = ga[:, 64:80]; msk = ga[:, 80:96]
            ex = ga[:, 96:112]; rw = ga[:, 112:128]
            p.stt("dve", lg, psL[:, 0:16], col[:, 8:9], gB, ALU.mult, ALU.subtract)
            p.red("dve", col[:, 9:10], lg, ALU.max)
            p.act(gated, lg, AF.Relu)
            p.ts("dve", pos, lg, 0.0, ALU.is_gt)
            p.red("dve", col[:, 10:11], pos, ALU.add)
            p.ts("dve", col[:, 11:12], col[:, 10:11], 0.5, ALU.is_lt)
            p.ts("dve", l2, lg, col[:, 9:10], ALU.is_ge)
            p.stt("dve", l2, l2, -1e30, lg, ALU.mult, ALU.add)
            p.red("dve", col[:, 12:13], l2, ALU.max)
            p.ts("dve", fb, lg, col[:, 12:13], ALU.is_ge)
            p.stt("dve", msk, fb, col[:, 11:12], pos, ALU.mult, ALU.add)
            p.act(ex, gated, AF.Exp)
            p.tt("dve", ex, ex, msk, ALU.mult)
            p.red("dve", col[:, 13:14], ex, ALU.add)
            p.recip(col[:, 14:15], col[:, 13:14])
            p.ts("dve", rw, ex, col[:, 14:15], ALU.mult)
            p.dma("qp", T["c_rw"][s, t, :, :], rw)
            p.copy("pool", rwb, rw)
            p.transpose(psB[0:16, 0:128], rwb[:, 0:16], identb)
            p.act(rwTb[0:16, :], psB[0:16, 0:128], AF.Identity)
            p.dma("qp", T["c_rwT"][s, :, t * 128:(t + 1) * 128], rwTb[0:16, :])




def ensure_masks(p):
    masks = p.sbufs.get("p1_masks")
    if masks is None:
        masks = p.sbuf("p1_masks", [128, 4 * 512], BF16)
        for i in range(4):
            sl = masks[:, i * 512:(i + 1) * 512]
            p.memset(sl, 0.0)
            p.affine_sel(sl, sl, ALU.is_ge, -1e9, -(i * 128), [[1, 512]], -1)
    return masks


def attn_core(p, T, qTa, kTa, v_of_kb, z_sink):
    onesb = p.const(T, "c_onesb")
    onesf = p.const(T, "c_ones")
    masks = ensure_masks(p)
    sqb = p.sbuf("p1_sqb", [128, 2048], BF16)
    rowf = p.sbuf("p1_rowf", [128, 2048], F32)
    cnb = sqb[0:1, 0:2048]
    mkc = p.sbuf("p1_mkc", [1, 2], F32)
    rcf = p.sbuf("p1_rcf", [128, 512], F32)
    rcB = p.sbuf("p1_rcB", [64, 512], F32)
    znb = p.sbuf("p1_znb", [64, 512], BF16)
    pT = [p.sbuf(f"p1_pT{i}", [128, 512], BF16) for i in range(2)]
    pT += [sqb[:, 0:512], sqb[:, 1024:1536]]
    PS = [p.bank(i) for i in range(6)]
    SC = [PS[0], PS[1], PS[4], PS[5]]
    p.act(sqb[0:64, :], qTa[0:64, :], AF.Square)
    for tb in range(4):
        p.mm(PS[3][0:1, 0:512], onesb[0:64, 0:1], sqb[0:64, tb * 512:(tb + 1) * 512])
        p.act(rowf[0:1, tb * 512:(tb + 1) * 512], PS[3][0:1, 0:512], AF.Sqrt)
    p.act(sqb[0:64, :], kTa[0:64, :], AF.Square)
    for tb in range(4):
        p.mm(PS[3][0:1, 0:512], onesb[0:64, 0:1], sqb[0:64, tb * 512:(tb + 1) * 512])
        p.act(rowf[64:65, tb * 512:(tb + 1) * 512], PS[3][0:1, 0:512], AF.Sqrt)
    p.red("dve", mkc[0:1, 0:1], rowf[64:65, :], ALU.max)
    p.ts("dve", mkc[0:1, 1:2], mkc[0:1, 0:1], -1.0, ALU.mult)
    p.ts("dve", rowf[0:1, :], rowf[0:1, :], mkc[0:1, 1:2], ALU.mult)
    p.copy("pool", cnb, rowf[0:1, :])
    for tb in range(4):
        p.mm(PS[3], onesb[0:1, 0:128], cnb[0:1, tb * 512:(tb + 1) * 512])
        p.act(qTa[64:65, tb * 512:(tb + 1) * 512], PS[3][64:65, 0:512], AF.Identity)
    for sb in range(4):
        kbmax = 4 * sb + 4

        def score(kb):
            ps = SC[kb % 4]
            p.mm(ps, kTa[0:65, kb * 128:(kb + 1) * 128],
                 qTa[0:65, sb * 512:(sb + 1) * 512])
            if kb >= 4 * sb:
                mi = kb - 4 * sb
                p.tt("dve", ps[:, 0:512], ps[:, 0:512],
                     masks[:, mi * 512:(mi + 1) * 512], ALU.add)
            p.act(pT[kb % 4], ps, AF.Exp)

        score(0)
        if kbmax > 1:
            score(1)
        for kb in range(kbmax):
            if kb + 2 < kbmax:
                score(kb + 2)
            p.mm(PS[2][0:65, :], v_of_kb(kb), pT[kb % 4],
                 start=(kb == 0), stop=(kb == kbmax - 1))
        p.recip(rcf[64:65, 0:512], PS[2][64:65, 0:512])
        p.mm(PS[3][0:64, 0:512], onesf[64:65, 0:64], rcf[64:65, 0:512])
        p.act(rcB, PS[3][0:64, 0:512], AF.Identity)
        p.tt("dve", znb, PS[2][0:64, :], rcB, ALU.mult)
        z_sink(sb, znb)



def emit_p1(p, T):
    identb = p.const(T, "c_identb")
    onesb = p.const(T, "c_onesb")
    onesf = p.const(T, "c_ones")
    W16 = p.sbuf("W16", [128, 24576], BF16)
    wq = stream_weight_b16(p, T, "wqT", W16, 0, 1024, 1024)
    wk = stream_weight_b16(p, T, "wkT", W16, 8192, 1024, 1024)
    wv = stream_weight_b16(p, T, "wvT", W16, 16384, 1024, 1024)
    xnT = p.sbuf("A0", [128, 16384], BF16)
    vv = p.sbuf("A1", [128, 16640], BF16)
    qTa = p.sbuf("p1_qT", [128, 2048], BF16)
    kTa = p.sbuf("p1_kT", [128, 2048], BF16)
    PS = [p.bank(i) for i in range(6)]

    for s in range(B_PER):
        for k in range(8):
            p.dma("qs", xnT[:, k * 2048:(k + 1) * 2048], T["c_xnT"][s, k])
        p.memset(vv[:, 0:16640].rearrange("p (t c) -> p t c", c=65)[:, :, 64:65], 1.0)
        for tt in range(16):
            for half in range(2):
                ps = PS[4 + half]
                p.mm_group(ps, [(xnT[:, k * 2048 + tt * 128: k * 2048 + (tt + 1) * 128],
                                 wv(k, half * 512, (half + 1) * 512)) for k in range(8)])
                src = ps[:, 0:512].rearrange("p (h c) -> p h c", h=8, c=64)
                dst_base = tt * 1040 + half * 8 * 65
                dst = vv[:, dst_base:dst_base + 8 * 65].rearrange(
                    "p (h c) -> p h c", h=8, c=65)[:, :, 0:64]
                p.act(dst, src, AF.Identity)
        for h in range(16):
            for tb in range(4):
                ps = PS[4 + (tb % 2)]
                p.mm_group(ps[0:64, :],
                           [(wq(k, h * 64, (h + 1) * 64),
                             xnT[:, k * 2048 + tb * 512: k * 2048 + (tb + 1) * 512])
                            for k in range(8)])
                p.act(qTa[0:64, tb * 512:(tb + 1) * 512], ps[0:64, :], AF.Identity,
                      scale=0.125)
            for tb in range(4):
                ps = PS[4 + (tb % 2)]
                p.mm_group(ps[0:64, :],
                           [(wk(k, h * 64, (h + 1) * 64),
                             xnT[:, k * 2048 + tb * 512: k * 2048 + (tb + 1) * 512])
                            for k in range(8)])
                p.act(kTa[0:64, tb * 512:(tb + 1) * 512], ps[0:64, :], AF.Identity)
            p.memset(kTa[64:65, :], 1.0)
            attn_core(p, T, qTa, kTa,
                      lambda kb: vv[:, kb * 1040 + h * 65: kb * 1040 + (h + 1) * 65],
                      lambda sb, znb_: p.dma(
                          "qp", T["c_zflatT"][s, h // 2, (h % 2) * 64:(h % 2) * 64 + 64,
                                              sb * 512:(sb + 1) * 512], znb_))


def emit_p1b(p, T):
    W16 = p.sbuf("W16", [128, 24576], BF16)
    wo = stream_weight_b16(p, T, "woT", W16, 0, 1024, 1024)
    zf = p.sbuf("A1", [128, 16640], BF16)
    swT = [p.sbuf(f"p1_swT{i}", [128, 512], F32) for i in range(2)]
    PS = [p.bank(i) for i in range(6)]
    for s in range(B_PER):
        for tb in range(4):
            half = (tb % 2) * 4096
            for k in range(8):
                p.dma("qs", zf[:, half + k * 512:half + (k + 1) * 512],
                      T["c_zflatT"][s, k, :, tb * 512:(tb + 1) * 512])
            for oc in range(8):
                ps = PS[4 + (oc % 2)]
                p.mm_group(ps, [(wo(k, oc * 128, (oc + 1) * 128),
                                 zf[:, half + k * 512:half + (k + 1) * 512])
                                for k in range(8)])
                p.act(swT[oc % 2], ps, AF.Identity)
                p.dma("qp", T["c_s0T"][s, oc, :, tb * 512:(tb + 1) * 512], swT[oc % 2])



def emit_p2(p, T):
    ident = p.const(T, "c_ident")
    W16 = p.sbuf("W16", [128, 24576], BF16)
    qkv = stream_weight_b16(p, T, "qkv_moe", W16, 0, 1024, 3072)
    xnT = p.sbuf("A0", [128, 16384], BF16)
    vm = p.sbuf("A1", [128, 16640], BF16)
    qmT = p.sbuf("p1_qT", [128, 2048], BF16)
    kmT = p.sbuf("p1_kT", [128, 2048], BF16)
    rwall = p.sbuf("p2_rwall", [128, 256], F32)
    acc = [p.sbuf(f"p2_acc{i}", [128, 192], F32) for i in range(2)]
    acc2 = [p.sbuf("p1_rcf", [128, 512], F32)[:, 0:192],
            p.sbuf("p1_rcf", [128, 512], F32)[:, 192:384]]
    PS = [p.bank(i) for i in range(6)]
    PSE = p.psum("PSE", [128, 3072])
    PSC = p.psum("PSC", [128, 512], F32)

    for s in range(B_PER):
        for k in range(8):
            p.dma("qs", xnT[:, k * 2048:(k + 1) * 2048], T["c_xnT"][s, k])
        for tt in range(16):
            p.dma("qp", rwall[:, tt * 16:(tt + 1) * 16], T["c_rw"][s, tt])
        p.memset(vm[:, 0:1040].rearrange("p (t c) -> p t c", c=65)[:, :, 64:65], 1.0)
        for tt in range(16):
            for k in range(8):
                for nb in range(6):
                    p.mm(PS[nb], xnT[:, k * 2048 + tt * 128: k * 2048 + (tt + 1) * 128],
                         qkv(k, nb * 512, (nb + 1) * 512), start=(k == 0), stop=(k == 7))
            a = acc[tt % 2]
            ab = acc2[tt % 2]
            p.ts("dve", a, PSE[:, 0:192], rwall[:, tt * 16:tt * 16 + 1], ALU.mult)
            p.ts("dve", ab, PSE[:, 192:384], rwall[:, tt * 16 + 1:tt * 16 + 2], ALU.mult)
            for e in range(2, 16, 2):
                p.stt("dve", a, PSE[:, e * 192:(e + 1) * 192],
                      rwall[:, tt * 16 + e:tt * 16 + e + 1], a, ALU.mult, ALU.add)
                p.stt("dve", ab, PSE[:, (e + 1) * 192:(e + 2) * 192],
                      rwall[:, tt * 16 + e + 1:tt * 16 + e + 2], ab, ALU.mult, ALU.add)
            p.tt("dve", a, a, ab, ALU.add)
            p.transpose(PSC[0:64, 0:128], a[:, 0:64], ident)
            p.act(qmT[0:64, tt * 128:(tt + 1) * 128], PSC[0:64, 0:128], AF.Identity,
                  scale=0.125)
            p.transpose(PSC[0:64, 128:256], a[:, 64:128], ident)
            p.act(kmT[0:64, tt * 128:(tt + 1) * 128], PSC[0:64, 128:256], AF.Identity)
            p.act(vm[:, tt * 65:tt * 65 + 64], a[:, 128:192], AF.Identity)
        p.memset(kmT[64:65, :], 1.0)
        attn_core(p, T, qmT, kmT, lambda kb: vm[:, kb * 65:(kb + 1) * 65],
                  lambda sb, znb_: p.dma("qp", T["c_amT"][s, :, sb * 512:(sb + 1) * 512],
                                         znb_))


def emit_p2b(p, T):
    ident = p.const(T, "c_ident")
    II = p.const(T, "c_II")
    _vm = p.sbuf("A1", [128, 16640], BF16)
    oh2 = _vm[0:16, 10240:11264]
    p.dma("qs", oh2, T["c_oh2"])
    W16 = p.sbuf("W16", [128, 24576], BF16)
    ost = stream_weight_b16(p, T, "o_stack", W16, 0, 1024, 1024)
    vm = p.sbuf("A1", [128, 16640], BF16)
    amT = vm[0:64, 8192:10240]
    rwTs = vm[0:16, 6144:8192]
    st2 = p.sbuf("p2_st2", [128, 512], BF16)
    rowf = p.sbuf("p1_rowf", [128, 2048], F32)
    g0c = rowf[:, 0:512]
    prodT = rowf[:, 512:1024]
    s0c = [rowf[:, 1024:1536], rowf[:, 1536:2048]]
    ptok = p.sbuf("AF0", [128, 4112], F32)
    xin = [p.sbuf(f"p0_x{i}", [128, 1024], F32) for i in range(2)]
    x1t = [p.sbuf("p0_xc", [128, 1024], F32), p.sbuf("p0_xn", [128, 1024], F32)]
    PS = [p.bank(i) for i in range(6)]

    for s in range(B_PER):
        p.dma("qp", amT, T["c_amT"][s])
        p.dma("qp", rwTs, T["c_rwT"][s])
        for tb in range(4):
            p.mm(PS[0], II[0:64, :], amT[0:64, tb * 512:(tb + 1) * 512])
            p.act(st2, PS[0], AF.Identity)
            for pr in range(8):
                p.mm(PS[1], oh2[0:16, pr * 128:(pr + 1) * 128],
                     rwTs[0:16, tb * 512:(tb + 1) * 512])
                p.tt("dve", vm[:, 2048 + pr * 512: 2048 + (pr + 1) * 512], PS[1], st2,
                     ALU.mult)
            for oc in range(8):
                ps = PS[2 + (oc % 2)]
                p.mm_group(ps, [(ost(k, oc * 128, (oc + 1) * 128),
                                 vm[:, 2048 + k * 512:2048 + (k + 1) * 512])
                                for k in range(8)])
                sc = s0c[oc % 2]
                p.dma("qs", sc, T["c_s0T"][s, oc, :, tb * 512:(tb + 1) * 512])
                p.act(g0c, ps, AF.Silu)
                p.tt("dve", prodT, g0c, sc, ALU.mult)
                for j in range(4):
                    p.transpose(PS[4 + (j % 2)][:, 0:128], prodT[:, j * 128:(j + 1) * 128],
                                ident)
                    p.act(ptok[:, j * 1024 + oc * 128: j * 1024 + (oc + 1) * 128],
                          PS[4 + (j % 2)][:, 0:128], AF.Identity)
            for j in range(4):
                xt_ = xin[j % 2]
                tg = tb * 4 + j
                p.dma("qs", xt_, T["x"][s, tg * 128:(tg + 1) * 128, :])
                p.tt("dve", x1t[j % 2], ptok[:, j * 1024:(j + 1) * 1024], xt_, ALU.add)
                p.dma("qp", T["c_x1"][s, tg * 128:(tg + 1) * 128, :], x1t[j % 2])



def emit_p3(p, T):
    identb = p.const(T, "c_identb")
    xt = [p.sbuf(f"p0_x{i}", [128, 1024], F32) for i in range(2)]
    col = p.sbuf("p0_col", [128, 16], F32)
    xc = p.sbuf("p0_xc", [128, 1024], F32)
    sq = p.sbuf("p0_sq", [128, 1024], F32)
    xn = p.sbuf("p0_xn", [128, 1024], F32)
    A1 = p.sbuf("A1", [128, 16640], BF16)
    xnb = A1[:, 0:1024]
    ub = A1[:, 4096:5120]
    oT = [p.sbuf(f"p0_xnTb{i}", [128, 128], BF16) for i in range(2)]
    PSB = p.psum("PSB", [128, 1024], BF16)
    eps32 = float(np.finfo(np.float32).eps)
    for s in range(B_PER):
        for t in range(16):
            xb = xt[t % 2]
            p.dma("qs", xb, T["c_x1"][s, t * 128:(t + 1) * 128, :])
            p.red("dve", col[:, 0:1], xb, ALU.add)
            p.ts("dve", col[:, 1:2], col[:, 0:1], 1.0 / 1024, ALU.mult)
            p.ts("dve", xc, xb, col[:, 1:2], ALU.subtract)
            p.act(sq, xc, AF.Square)
            p.red("dve", col[:, 2:3], sq, ALU.add)
            p.act(col[:, 3:4], col[:, 2:3], AF.Sqrt, scale=1.0 / 1024, bias=p.fbias(1e-5))
            p.recip(col[:, 4:5], col[:, 3:4])
            p.ts("dve", xn, xc, col[:, 4:5], ALU.mult)
            p.dma("qp", T["c_xn1"][s, t * 128:(t + 1) * 128, :], xn)
            p.act(sq, xn, AF.Square)
            p.red("dve", col[:, 5:6], sq, ALU.add)
            p.act(col[:, 6:7], col[:, 5:6], AF.Sqrt, scale=1.0 / 1024, bias=p.fbias(eps32))
            p.recip(col[:, 7:8], col[:, 6:7])
            p.act(xnb, xn, AF.Identity)
            p.ts("dve", xc, xn, col[:, 7:8], ALU.mult)
            p.act(ub, xc, AF.Identity)
            for k in range(8):
                p.transpose(PSB[:, 0:128], xnb[:, k * 128:(k + 1) * 128], identb)
                p.act(oT[0], PSB[:, 0:128], AF.Identity)
                p.dma("qp", T["c_xn1T"][s, k, :, t * 128:(t + 1) * 128], oT[0])
                p.transpose(PSB[:, 128:256], ub[:, k * 128:(k + 1) * 128], identb)
                p.act(oT[1], PSB[:, 128:256], AF.Identity)
                p.dma("qa", T["c_uT"][s, k, :, t * 128:(t + 1) * 128], oT[1])



def emit_p4(p, T):
    uT = p.sbuf("A0", [128, 16384], BF16)
    A1 = p.sbuf("A1", [128, 16640], BF16)
    xpad = p.sbuf("AF0", [128, 4112], F32)
    wstg = xpad[:, 2064:4112]
    wbuf = [A1[:, 4096 + i * 1024:4096 + (i + 1) * 1024] for i in range(4)]
    convw = p.sbuf("p4_cw", [128, 72], F32)
    convb = p.sbuf("p4_cb", [128, 18], F32)
    zxb = A1[:, 0:2048]
    xbc_pad = xpad[:, 0:2051]
    convo = p.sbuf("p5_ytok", [128, 2048], F32)
    dtr = p.sbuf("p1_rowf", [128, 2048], F32)
    dts = convo
    dtb_col = p.sbuf("p4_dtb", [32, 1], F32)
    alog = p.sbuf("p4_alog", [32, 2], F32)
    PS = [p.bank(i) for i in range(6)]
    p.dma("qs", dtb_col, T["dt_bias"])
    p.dma("qs", alog[:, 0:1], T["A_log"])
    p.act(alog[0:32, 1:2], alog[0:32, 0:1], AF.Exp)
    p.ts("dve", alog[0:32, 1:2], alog[0:32, 1:2], -1.0, ALU.mult)
    for cc in range(18):
        p.dma("qs", convw[:, cc * 4:(cc + 1) * 4], T["conv_w"][cc * 128:(cc + 1) * 128, :])
        p.dma("qs", convb[:, cc:cc + 1], T["conv_b"][cc * 128:(cc + 1) * 128, :])
    for s in range(B_PER):
        for k in range(8):
            p.dma("qs", uT[:, k * 2048:(k + 1) * 2048], T["c_uT"][s, k])
        for oc in range(35):
            wcols = 128 if oc < 34 else 32
            w = wbuf[oc % 4]
            half = (oc % 2) * 1024
            for k in range(8):
                p.dma("qs", wstg[:, half + k * 128:half + k * 128 + wcols],
                      T["inprojT"][k * 128:(k + 1) * 128, oc * 128:oc * 128 + wcols])
            if wcols == 128:
                if oc % 2:
                    p.copy("dve", w[:, 0:1024], wstg[:, half:half + 1024])
                else:
                    p.act(w[:, 0:1024], wstg[:, half:half + 1024], AF.Identity)
            else:
                p.copy("pool", w[:, 0:256].rearrange("p (k c) -> p k c", k=8),
                       wstg[:, half:half + 1024].rearrange("p (k c) -> p k c", k=8)[:, :, 0:32])
            for tb in range(4):
                ps = PS[4 + (tb % 2)]
                p.mm_group(ps[0:wcols, 0:512],
                           [(w[:, k * wcols:(k + 1) * wcols],
                             uT[:, k * 2048 + tb * 512:k * 2048 + (tb + 1) * 512])
                            for k in range(8)])
                if oc < 16:
                    p.act(zxb[:, tb * 512:(tb + 1) * 512], ps[:, 0:512], AF.Identity)
                elif oc < 34:
                    p.act(xbc_pad[:, 3 + tb * 512:3 + (tb + 1) * 512], ps[:, 0:512],
                          AF.Identity)
                else:
                    p.act(dtr[0:32, tb * 512:(tb + 1) * 512], ps[0:32, 0:512],
                          AF.Identity)
            if oc < 16:
                p.dma("qp", T["c_zT"][s, oc], zxb)
            elif oc < 34:
                cc = oc - 16
                p.memset(xbc_pad[:, 0:3], 0.0)
                p.ts("dve", convo, xbc_pad[:, 0:2048], convw[:, cc * 4:cc * 4 + 1],
                     ALU.mult)
                for j in range(1, 4):
                    p.stt("dve", convo, xbc_pad[:, j:j + 2048],
                          convw[:, cc * 4 + j:cc * 4 + j + 1], convo, ALU.mult, ALU.add)
                p.act(zxb, convo, AF.Silu, bias=convb[:, cc:cc + 1])
                p.dma("qp", T["c_xbcT"][s, cc], zxb)
        p.act(dts[0:32, :], dtr[0:32, :], AF.Abs, bias=dtb_col)
        p.act(dts[0:32, :], dts[0:32, :], AF.Exp, scale=-1.0)
        p.act(dts[0:32, :], dts[0:32, :], AF.Ln, bias=p.fbias(1.0, 32))
        p.act(dtr[0:32, :], dtr[0:32, :], AF.Relu, bias=dtb_col)
        p.tt("dve", dtr[0:32, :], dtr[0:32, :], dts[0:32, :], ALU.add)
        p.dma("qp", T["c_dt"][s], dtr[0:32, :])
        p.ts("dve", dts[0:32, :], dtr[0:32, :], alog[0:32, 1:2], ALU.mult)
        p.dma("qp", T["c_dtA"][s], dts[0:32, :])



def emit_p5a(p, T):
    ident = p.const(T, "c_ident")
    identb = p.const(T, "c_identb")
    onesf = p.const(T, "c_ones")
    trilU = p.const(T, "c_trilU")
    maskT = p.const(T, "c_maskT")
    e127 = p.const(T, "c_e127")
    W16 = p.sbuf("W16", [128, 24576], BF16)
    opw = stream_weight_b16(p, T, "outprojT", W16, 0, 2048, 1024)
    rows = p.sbuf("p1_rowf", [128, 2048], F32)[0:1, 0:2048]
    nwB = [p.sbuf("p1_kT", [128, 2048], BF16)[:, :].bitcast(F32),
           p.sbuf("p1_sqb", [128, 2048], BF16)[:, :].bitcast(F32)]
    msB = p.sbuf("p1_qT", [128, 2048], BF16)[:, :].bitcast(F32)
    dB = p.sbuf("p5_dB", [128, 64], F32)
    PS = [p.bank(i) for i in range(6)]
    PSC = p.psum("PSC", [128, 512], F32)
    PSB = p.psum("PSB", [128, 1024], BF16)
    p.dma("qs", rows, T["norm_w"])
    for i in range(4):
        p.mm(PS[4], onesf[0:1, 0:128], rows[0:1, i * 512:(i + 1) * 512])
        p.act(nwB[i // 2][:, (i % 2) * 512:(i % 2 + 1) * 512], PS[4], AF.Identity)
    p.dma("qs", rows, T["D_rep"])
    p.mm(PS[4][:, 0:32], onesf[0:1, 0:128],
         rows[0:1, 0:2048].rearrange("o (h j) -> o h j", j=64)[:, :, 0:1])
    p.act(dB[:, 0:32], PS[4][:, 0:32], AF.Identity)
    p.dma("qs", rows[0:1, 0:1024], T["mixer_scale"])
    for i in range(2):
        p.mm(PS[4], onesf[0:1, 0:128], rows[0:1, i * 512:(i + 1) * 512])
        p.act(msB[:, i * 512:(i + 1) * 512], PS[4], AF.Identity)

    A0 = p.sbuf("A0", [128, 16384], BF16)
    A1 = p.sbuf("A1", [128, 16640], BF16)
    wstg = p.sbuf("wstg", [128, 2 * 512], F32)
    CB = A1[:, 10752:12160].bitcast(F32)
    DTC = p.sbuf("p1_rcB", [64, 512], F32)
    PAR = []
    for par in range(2):
        A = A0 if par == 0 else A1
        pb = par * 352
        PAR.append(dict(
            xt=A[:, 0:2048], z=A[:, 2048:4096],
            BT=A[:, 4096:4224], CT=A[:, 4224:4352], Btok=A[:, 4352:4480],
            X0=A[:, 4608:6656], X1=A[:, 6656:8704], X2=A[:, 8704:10752],
            gtm=(A0[:, 10880:11008] if par == 0 else A1[:, 4480:4608]),
            dtAT=CB[:, pb + 0:pb + 32], dtT=CB[:, pb + 32:pb + 64],
            cum=CB[:, pb + 64:pb + 96], cumn=CB[:, pb + 96:pb + 128],
            e1=CB[:, pb + 128:pb + 160], e2=CB[:, pb + 160:pb + 192],
            lamB=CB[:, pb + 192:pb + 224],
            cqr=CB[0:1, pb + 224:pb + 256], cqe=CB[0:1, pb + 256:pb + 288],
            dr=par * 256,
            pcs=par * 256,
            yt=([p.sbuf("p5_ytok", [128, 2048], F32)] if par == 0 else
                [wstg[:, 0:1024], A1[:, 13184:15232].bitcast(F32)]),
        ))

    def ytv(P, c0, c1):
        ys = P["yt"]
        if len(ys) == 1:
            return ys[0][:, c0:c1]
        if c1 <= 1024:
            return ys[0][:, c0:c1]
        return ys[1][:, c0 - 1024:c1 - 1024]

    def yt_halves(P):
        return [ytv(P, 0, 1024), ytv(P, 1024, 2048)]

    gmt = [A0[:, 10752:10880], A0[:, 11136:11264]]
    eh = [A0[:, 11008:11136], A0[:, 11264:11392]]
    rbt2 = [p.sbuf("p2_st2", [128, 512], BF16)[:, :].bitcast(F32)[:, 0:128],
            A0[:, 11392:11648].bitcast(F32)]
    PS5T = [PS[5][:, 128:256], PS[5][:, 256:384]]
    ynT = A0[:, 12288:14336]
    ynb = A0[:, 14336:16384]
    hT = p.sbuf("p1_rowf", [128, 2048], F32)
    hT16 = p.sbuf("p1_masks", [128, 4 * 512], BF16)
    zsil = [p.sbuf("p1_rcf", [128, 512], F32), p.sbuf("p1_swT0", [128, 512], F32)]
    sqa = p.sbuf("AF0", [128, 4112], F32)
    _st2 = p.sbuf("p2_st2", [128, 512], BF16)[:, :].bitcast(F32)
    tmp64 = [_st2[:, 128:192], _st2[:, 192:256]]
    xp_t = p.sbuf("p0_xc", [128, 1024], F32)
    xn1t = p.sbuf("p0_x0", [128, 1024], F32)
    h2 = p.sbuf("p0_xn", [128, 1024], F32)
    h2b = A0[:, 14336:15360]
    col = p.sbuf("p0_col", [128, 16], F32)
    eps32 = float(np.finfo(np.float32).eps)

    def bc(t, n):
        return t[:, :].rearrange("p (h one) -> p h one", one=1).broadcast_to([128, 32, n])

    def emit_prep(s, c):
        P = PAR[(s * NCHUNK + c) % 2]
        b = P["pcs"]
        xt_fm, z_fm = P["xt"], P["z"]
        BT, CT, Btok = P["BT"], P["CT"], P["Btok"]
        X0, X1, X2 = P["X0"], P["X1"], P["X2"]
        gtm, dr = P["gtm"], P["dr"]
        cum, cumn = P["cum"], P["cumn"]
        e1, e2, lamB = P["e1"], P["e2"], P["lamB"]
        cs = slice(c * 128, (c + 1) * 128)
        xsrc = T["c_xbcT"].rearrange("s k p t -> s p k t")
        zsrc = T["c_zT"].rearrange("s k p t -> s p k t")
        for k4 in range(4):
            p.dma("qs", xt_fm[:, k4 * 512:(k4 + 1) * 512].rearrange(
                "p (k t) -> p k t", k=4),
                xsrc[s, :, k4 * 4:(k4 + 1) * 4, cs])
            p.dma("qa", z_fm[:, k4 * 512:(k4 + 1) * 512].rearrange(
                "p (k t) -> p k t", k=4),
                zsrc[s, :, k4 * 4:(k4 + 1) * 4, cs])
        p.dma("qs", BT, T["c_xbcT"][s, 16, :, cs])
        p.dma("qs", CT, T["c_xbcT"][s, 17, :, cs])
        p.dma("qs", DTC[0:32, dr:dr + 128], T["c_dt"][s, :, cs])
        p.dma("qs", DTC[0:32, dr + 128:dr + 256], T["c_dtA"][s, :, cs])
        p.transpose(PSC[:, b:b + 32], DTC[0:32, dr + 128:dr + 256], ident[0:32, 0:32])
        p.act(P["dtAT"], PSC[:, b:b + 32], AF.Identity)
        p.transpose(PSC[:, b + 32:b + 64], DTC[0:32, dr:dr + 128], ident[0:32, 0:32])
        p.act(P["dtT"], PSC[:, b + 32:b + 64], AF.Identity)
        p.mm(PSC[:, b + 64:b + 96], trilU, P["dtAT"])
        p.act(cum, PSC[:, b + 64:b + 96], AF.Identity)
        p.act(cumn, PSC[:, b + 64:b + 96], AF.Identity, scale=-1.0)
        p.act(e1, PSC[:, b + 64:b + 96], AF.Exp)
        p.mm(PSC[0:1, b + 96:b + 128], e127, cum)
        p.act(P["cqr"], PSC[0:1, b + 96:b + 128], AF.Identity)
        p.mm(PSC[:, b + 128:b + 160], onesf[0:1, 0:128], P["cqr"])
        p.tt("dve", e2, PSC[:, b + 128:b + 160], cum, ALU.subtract)
        p.act(e2, e2, AF.Exp)
        p.act(P["cqe"], P["cqr"], AF.Exp)
        p.mm(PSC[:, b + 160:b + 192], onesf[0:1, 0:128], P["cqe"])
        p.act(lamB, PSC[:, b + 160:b + 192], AF.Identity)
        for r in range(2):
            for k in range(8):
                p.transpose(PSB[:, k * 128:(k + 1) * 128],
                            xt_fm[:, (r * 8 + k) * 128:(r * 8 + k + 1) * 128], identb)
            p.act(X0[:, r * 1024:(r + 1) * 1024], PSB[:, :], AF.Identity)
        p.tt("pool", X1.rearrange("p (h j) -> p h j", h=32),
             X0.rearrange("p (h j) -> p h j", h=32), bc(P["dtT"], 64), ALU.mult)
        p.tt("pool", X2.rearrange("p (h j) -> p h j", h=32),
             X1.rearrange("p (h j) -> p h j", h=32), bc(e2, 64), ALU.mult)
        p.transpose(PSB[:, 0:128], BT, identb)
        p.act(Btok, PSB[:, 0:128], AF.Identity)
        p.mm(PS[4][:, 0:128], BT, CT)
        p.tt("dve", gtm, PS[4][:, 0:128], trilU, ALU.mult)

    def emit_rest(s, c):
        P = PAR[(s * NCHUNK + c) % 2]
        X0, X1, X2 = P["X0"], P["X1"], P["X2"]
        z_fm, CT, Btok, gtm = P["z"], P["CT"], P["Btok"], P["gtm"]
        cum, cumn = P["cum"], P["cumn"]
        e1, lamB = P["e1"], P["lamB"]
        cs = slice(c * 128, (c + 1) * 128)
        if c == 0:
            for ya in yt_halves(P):
                p.memset(ya, 0.0)
        else:
            for qd in range(4):
                p.mm(PS[qd], CT, hT16[:, qd * 512:(qd + 1) * 512])
                p.tt("dve", ytv(P, qd * 512, (qd + 1) * 512).rearrange(
                    "p (h j) -> p h j", h=8),
                    PS[qd][:, 0:512].rearrange("p (h j) -> p h j", h=8),
                    bc(e1, 64)[:, qd * 8:(qd + 1) * 8, :], ALU.mult)
        for h in range(NHEADS):
            i2 = h % 2
            p.act(rbt2[i2], maskT, AF.Identity, bias=cum[:, h:h + 1])
            p.transpose(PS5T[i2], rbt2[i2], ident)
            p.act(eh[i2], PS5T[i2], AF.Exp,
                  bias=cumn[:, h:h + 1])
            p.tt("pool", gmt[i2], gtm, eh[i2], ALU.mult)
            p.mm(PS[5][0:128, i2 * 64:(i2 + 1) * 64], gmt[i2],
                 X1[:, h * 64:(h + 1) * 64])
            p.stt("dve", tmp64[i2], X0[:, h * 64:(h + 1) * 64], dB[:, h:h + 1],
                  PS[5][:, i2 * 64:(i2 + 1) * 64], ALU.mult, ALU.add)
            p.tt("pool", ytv(P, h * 64, (h + 1) * 64), ytv(P, h * 64, (h + 1) * 64),
                 tmp64[i2], ALU.add)
        for qd in range(4):
            p.mm(PS[qd], Btok, X2[:, qd * 512:(qd + 1) * 512])
        for qd in range(4):
            p.tt("dve", hT[:, qd * 512:(qd + 1) * 512].rearrange(
                "p (h j) -> p h j", h=8),
                hT[:, qd * 512:(qd + 1) * 512].rearrange("p (h j) -> p h j", h=8),
                bc(lamB, 64)[:, qd * 8:(qd + 1) * 8, :], ALU.mult)
            p.tt("dve", hT[:, qd * 512:(qd + 1) * 512], hT[:, qd * 512:(qd + 1) * 512],
                 PS[qd][:, 0:512], ALU.add)
        p.act(hT16, hT, AF.Identity)
        for r in range(2):
            for k in range(8):
                p.transpose(PSB[:, k * 128:(k + 1) * 128],
                            z_fm[:, (r * 8 + k) * 128:(r * 8 + k + 1) * 128], identb)
            for hlf in range(2):
                zs = zsil[hlf]
                p.act(zs, PSB[:, hlf * 512:(hlf + 1) * 512], AF.Silu)
                p.tt("pool", ytv(P, r * 1024 + hlf * 512, r * 1024 + (hlf + 1) * 512),
                     ytv(P, r * 1024 + hlf * 512, r * 1024 + (hlf + 1) * 512),
                     zs, ALU.mult)
        yh = yt_halves(P)
        p.act(sqa[:, 0:1024], yh[0], AF.Square)
        p.act(sqa[:, 1024:2048], yh[1], AF.Square)
        p.red("dve", col[:, 0:1], sqa[:, 0:2048], ALU.add)
        p.act(col[:, 1:2], col[:, 0:1], AF.Sqrt, scale=1.0 / 2048, bias=p.fbias(1e-5))
        p.recip(col[:, 2:3], col[:, 1:2])
        p.ts("dve", yh[0], yh[0], col[:, 2:3], ALU.mult)
        p.ts("dve", yh[1], yh[1], col[:, 2:3], ALU.mult)
        p.tt("pool", yh[0], yh[0], nwB[0], ALU.mult)
        p.tt("pool", yh[1], yh[1], nwB[1], ALU.mult)
        p.act(ynb[:, 0:1024], yh[0], AF.Identity)
        p.act(ynb[:, 1024:2048], yh[1], AF.Identity)
        for r in range(2):
            for k in range(8):
                p.transpose(PSB[:, k * 128:(k + 1) * 128],
                            ynb[:, (r * 8 + k) * 128:(r * 8 + k + 1) * 128], identb)
            p.act(ynT[:, r * 1024:(r + 1) * 1024], PSB[:, :], AF.Identity)
        for half in range(2):
            p.mm_group(PS[4 + half],
                       [(ynT[:, d * 128:(d + 1) * 128],
                         opw(d, half * 512, (half + 1) * 512)) for d in range(16)])
        p.dma("qs", xn1t, T["c_xn1"][s, cs, :])
        for half in range(2):
            hs = slice(half * 512, (half + 1) * 512)
            p.tt("dve", xp_t[:, hs], PS[4 + half][:, 0:512], msB[:, hs], ALU.mult)
            p.tt("dve", xp_t[:, hs], xp_t[:, hs], xn1t[:, hs], ALU.add)
        p.dma("qs", T["c_xp"][s, cs, :], xp_t)
        p.act(sqa[:, 2048:3072], xp_t, AF.Square)
        p.red("dve", col[:, 3:4], sqa[:, 2048:3072], ALU.add)
        p.act(col[:, 4:5], col[:, 3:4], AF.Sqrt, scale=1.0 / 1024, bias=p.fbias(eps32))
        p.recip(col[:, 5:6], col[:, 4:5])
        p.ts("dve", h2, xp_t, col[:, 5:6], ALU.mult)
        p.act(h2b, h2, AF.Identity)
        for k in range(8):
            p.transpose(PSB[:, (k % 2) * 128:(k % 2) * 128 + 128],
                        h2b[:, k * 128:(k + 1) * 128], identb)
            p.act(gmt[k % 2], PSB[:, (k % 2) * 128:(k % 2) * 128 + 128], AF.Identity)
            p.dma("qs", T["c_h2T"][s, k, :, cs], gmt[k % 2])

    steps = [(s, c) for s in range(B_PER) for c in range(NCHUNK)]
    prev = None
    for st in steps + [None]:
        if st is not None:
            emit_prep(*st)
        if prev is not None:
            if prev[1] == 0:
                p.memset(hT, 0.0)
                p.memset(hT16, 0.0)
            emit_rest(*prev)
        prev = st




def emit_p5c(p, T):
    onesf = p.const(T, "c_ones")
    W16 = p.sbuf("W16", [128, 24576], BF16)
    h2T = p.sbuf("A0", [128, 16384], BF16)
    A1 = p.sbuf("A1", [128, 16640], BF16)
    wstg = p.sbuf("AF0", [128, 4112], F32)[:, 2064:4112]
    wg = [A1[:, 8192 + i * 1024:8192 + (i + 1) * 1024] for i in range(4)]
    wu = [A1[:, 12288 + i * 1024:12288 + (i + 1) * 1024] for i in range(4)]
    silg = p.sbuf("p1_rowf", [128, 2048], F32)
    midc = [p.sbuf("p2_st2", [128, 512], BF16), p.sbuf("p1_pT0", [128, 512], BF16)]
    midbuf = [A1[:, i * 4096:(i + 1) * 4096] for i in range(2)]
    mlsrow = p.sbuf("p1_rowf", [128, 2048], F32)[0:1, 0:1024]
    mlsB = p.sbuf("p1_qT", [128, 2048], BF16)[:, :].bitcast(F32)
    xpt = p.sbuf("p0_x0", [128, 1024], F32)
    s1t = p.sbuf("p0_xc", [128, 1024], F32)
    PS = [p.bank(i) for i in range(6)]
    p.dma("qs", mlsrow, T["mlp_scale"])
    for i in range(2):
        p.mm(PS[4], onesf[0:1, 0:128], mlsrow[0:1, i * 512:(i + 1) * 512])
        p.act(mlsB[:, i * 512:(i + 1) * 512], PS[4], AF.Identity)
    for s in range(B_PER):
        for k in range(8):
            p.dma("qs", h2T[:, k * 2048:(k + 1) * 2048], T["c_h2T"][s, k])
        for m in range(32):
            g = wg[m % 4]
            u = wu[m % 4]
            half = (m % 2) * 1024
            for k in range(8):
                p.dma("qs", wstg[:, half + k * 128:half + (k + 1) * 128],
                      T["gateT"][k * 128:(k + 1) * 128, m * 128:(m + 1) * 128])
            p.copy("pool", g, wstg[:, half:half + 1024])
            for k in range(8):
                p.dma("qa", wstg[:, half + k * 128:half + (k + 1) * 128],
                      T["upT"][k * 128:(k + 1) * 128, m * 128:(m + 1) * 128])
            p.copy("dve", u, wstg[:, half:half + 1024])
            for tb in range(4):
                p.mm_group(PS[4], [(g[:, k * 128:(k + 1) * 128],
                                    h2T[:, k * 2048 + tb * 512:k * 2048 + (tb + 1) * 512])
                                   for k in range(8)])
                p.mm_group(PS[5], [(u[:, k * 128:(k + 1) * 128],
                                    h2T[:, k * 2048 + tb * 512:k * 2048 + (tb + 1) * 512])
                                   for k in range(8)])
                p.act(silg[:, (tb % 2) * 512:(tb % 2) * 512 + 512], PS[4], AF.Silu)
                p.tt("dve", midc[tb % 2], silg[:, (tb % 2) * 512:(tb % 2) * 512 + 512],
                     PS[5], ALU.mult)
                p.dma("qp", T["c_midT"][m, :, s * 2048 + tb * 512:s * 2048 + (tb + 1) * 512],
                      midc[tb % 2])
    for half in range(2):
        dw = stream_weight_b16(p, T, "downT", W16, 0, 4096, 512) if half == 0 else None
        if half == 1:
            dw = _stream_down_half(p, T, W16)
        for s in range(B_PER):
            for tq in range(16):
                mb = midbuf[tq % 2]
                src = T["c_midT"].rearrange("d p t -> p d t")
                for g8 in range(8):
                    p.dma("qs", mb[:, g8 * 512:(g8 + 1) * 512].rearrange(
                        "p (d t) -> p d t", d=4),
                        src[:, g8 * 4:(g8 + 1) * 4,
                            s * 2048 + tq * 128:s * 2048 + (tq + 1) * 128])
                ps = PS[tq % 2]
                p.mm_group(ps, [(mb[:, d * 128:(d + 1) * 128], dw(d, 0, 512))
                                for d in range(32)])
                hs = slice(half * 512, (half + 1) * 512)
                p.dma("qs", xpt[:, hs], T["c_xp"][s, tq * 128:(tq + 1) * 128, hs])
                p.tt("dve", s1t[:, hs], ps[:, 0:512], mlsB[:, hs], ALU.mult)
                p.tt("dve", s1t[:, hs], s1t[:, hs], xpt[:, hs], ALU.add)
                p.dma("qp", T["c_s1"][s, tq * 128:(tq + 1) * 128, hs], s1t[:, hs])


def _stream_down_half(p, T, W16):
    stg = p.sbuf("wstg", [128, 2 * 512], F32)
    src = T["downT"]
    for k in range(32):
        half = (k % 2) * 512
        p.dma("qs", stg[:, half:half + 512], src[k * 128:(k + 1) * 128, 512:1024])
        p.copy("pool", W16[:, k * 512:(k + 1) * 512], stg[:, half:half + 512])

    def chunk(k, c0, c1):
        return W16[:, k * 512:(k + 1) * 512]
    return chunk



def emit_p5d(p, T):
    selp = p.const(T, "c_selpool")
    W16 = p.sbuf("W16", [128, 24576], BF16)
    fw = stream_weight_b16(p, T, "l1fwT", W16, 0, 1024, 1024)
    dstg = p.sbuf("AF0", [128, 4112], F32)[:, 2064:4112]
    dish = p.sbuf("A1", [128, 16640], BF16)[0:64, 2048:3072]
    p.dma("qs", dstg[0:64, 0:1024], T["disha"])
    p.copy("pool", dish, dstg[0:64, 0:1024])
    xn1T = p.sbuf("A0", [128, 16384], BF16)
    A1 = p.sbuf("A1", [128, 16640], BF16)
    pooledT = A1[0:64, 0:2048]
    g1c = p.sbuf("p1_rowf", [128, 2048], F32)
    PS = [p.bank(i) for i in range(6)]
    for s in range(B_PER):
        for k in range(8):
            p.dma("qs", xn1T[:, k * 2048:(k + 1) * 2048], T["c_xn1T"][s, k])
        for tb in range(4):
            p.mm_group(PS[4][0:64, 0:512],
                       [(selp[:, 0:64], xn1T[:, k * 2048 + tb * 512:k * 2048 + (tb + 1) * 512])
                        for k in range(8)])
            p.act(pooledT[0:64, tb * 512:(tb + 1) * 512], PS[4][0:64, 0:512], AF.Identity)
        for tb in range(4):
            for oc in range(8):
                ps = PS[oc % 2]
                pairs = [(fw(k, oc * 128, (oc + 1) * 128),
                          xn1T[:, k * 2048 + tb * 512:k * 2048 + (tb + 1) * 512])
                         for k in range(8)]
                pairs.append((dish[0:64, oc * 128:(oc + 1) * 128],
                              pooledT[0:64, tb * 512:(tb + 1) * 512]))
                p.mm_group(ps, pairs)
                p.act(g1c[:, (oc % 4) * 512:(oc % 4 + 1) * 512], ps, AF.Silu)
                p.dma("qp", T["c_g1T"][s, oc, :, tb * 512:(tb + 1) * 512],
                      g1c[:, (oc % 4) * 512:(oc % 4 + 1) * 512])



def emit_p5e(p, T):
    ident = p.const(T, "c_ident")
    gf = [p.sbuf(f"p5e_gf{i}", [128, 128], F32) for i in range(2)]
    gtok = p.sbuf("AF0", [128, 4112], F32)
    x1t = [p.sbuf(f"p0_x{i}", [128, 1024], F32) for i in range(2)]
    s1t = [p.sbuf("p0_xc", [128, 1024], F32), p.sbuf("p0_xn", [128, 1024], F32)]
    ot = p.sbuf("p0_sq", [128, 1024], F32)
    PSC = p.psum("PSC", [128, 512], F32)
    for s in range(B_PER):
        for tq in range(16):
            gslot = (tq % 4) * 1024
            for k in range(8):
                p.dma("qs", gf[k % 2], T["c_g1T"][s, k, :, tq * 128:(tq + 1) * 128])
                p.transpose(PSC[:, (k % 2) * 128:(k % 2) * 128 + 128], gf[k % 2], ident)
                p.act(gtok[:, gslot + k * 128:gslot + (k + 1) * 128],
                      PSC[:, (k % 2) * 128:(k % 2) * 128 + 128], AF.Identity)
            p.dma("qp", x1t[tq % 2], T["c_x1"][s, tq * 128:(tq + 1) * 128, :])
            p.dma("qa", s1t[tq % 2], T["c_s1"][s, tq * 128:(tq + 1) * 128, :])
            p.tt("dve", ot, s1t[tq % 2], gtok[:, gslot:gslot + 1024], ALU.mult)
            p.tt("dve", ot, ot, x1t[tq % 2], ALU.add)
            p.dma("qs", T["y"][s, tq * 128:(tq + 1) * 128, :], ot)


EMITTERS = {"p0": emit_p0, "p1": emit_p1, "p1b": emit_p1b,
            "p2": emit_p2, "p2b": emit_p2b, "p3": emit_p3, "p4": emit_p4,
            "p5a": emit_p5a, "p5c": emit_p5c, "p5d": emit_p5d, "p5e": emit_p5e}


def _scratch_kind(name, phases, keep=()):
    w = any(name in PHASE_WRITES[ph] for ph in phases)
    r = any(name in PHASE_READS[ph] for ph in phases)
    if w:
        consumed_outside = any(name in PHASE_READS[ph] for ph in PHASE_READS
                               if ph not in phases)
        if consumed_outside or name in keep:
            return "ExternalOutput"
        return "Internal"
    if r:
        return "ExternalInput"
    return None


def build_launch(phases, keep=()):
    nc = bass.Bass("TRN2", target_bir_lowering=False, debug=False)
    stack = ExitStack()
    T = {}
    needs_x = any(ph in PHASE_NEEDS_X for ph in phases)
    T["x"] = nc.dram_tensor("x", [B_PER, S, D], F32, kind="ExternalInput")
    for name, (shape, dt) in WEIGHT_SPECS.items():
        T[name] = nc.dram_tensor(name, shape, dt, kind="ExternalInput")
    for name, (shape, dt) in SCRATCH_SPECS.items():
        kind = _scratch_kind(name, phases, keep)
        if kind is not None:
            T[name] = nc.dram_tensor(name, shape, dt, kind=kind)
    if any(ph in PHASE_WRITES_OUT for ph in phases):
        T["y"] = nc.dram_tensor("y", [B_PER, S, D], F32, kind="ExternalOutput")
    p = Prog(nc, stack)
    for ph in phases:
        EMITTERS[ph](p, T)
    p.finish()
    return nc, stack, T


def _ext_names(nc):
    ext_in, ext_out = [], []
    skip = {getattr(nc.partition_id_tensor, "name", None), "dbg_addr"}
    for alloc in nc.m.functions[0].allocations:
        if not isinstance(alloc, mybir.MemoryLocationSet):
            continue
        nm = alloc.memorylocations[0].name
        if nm in skip:
            continue
        if alloc.kind == "ExternalInput":
            ext_in.append(nm)
        elif alloc.kind == "ExternalOutput":
            ext_out.append(nm)
    return ext_in, ext_out


def run_groups(inputs, groups, keep=()):
    prep = prep_inputs(inputs)
    x = np.asarray(inputs["L0_attn_x"])
    per_core_state = []
    for c in range(NCORE):
        st = dict(prep)
        st["x"] = np.ascontiguousarray(x[c * B_PER:(c + 1) * B_PER])
        per_core_state.append(st)
    for group in groups:
        nc, stack, T = build_launch(group, keep=keep)
        ext_in, ext_out = _ext_names(nc)
        in_maps = [{n: per_core_state[c][n] for n in ext_in} for c in range(NCORE)]
        res = run_bass_kernel_spmd(nc, in_maps, core_ids=list(range(NCORE)))
        for c in range(NCORE):
            for n in ext_out:
                per_core_state[c][n] = res.results[c][n]
        stack.close()
    return per_core_state


def kernel(**inputs):
    state = run_groups(inputs, GROUPS)
    ys = [state[c]["y"] for c in range(NCORE)]
    return np.concatenate(ys, axis=0).astype(np.float32)


# version 09
# speedup vs baseline: 1.0293x; 1.0293x over previous
import numpy as np
import ml_dtypes
from contextlib import ExitStack

import concourse.bass as bass
import concourse.mybir as mybir
from concourse.bass_utils import run_bass_kernel_spmd

F32 = mybir.dt.float32
BF16 = mybir.dt.bfloat16
AF = mybir.ActivationFunctionType
ALU = mybir.AluOpType
AX = mybir.AxisListType

NCORE = 8
B_PER = 2
S = 2048
D = 1024
NH = 16
DH = 64
NE = 16
D_INNER = 2048
D_STATE = 128
NHEADS = 32
HEADDIM = 64
CONV_DIM = 2304
D_IN_PROJ = 4384
HIDDEN = 4096
Q = 128
NCHUNK = S // Q


COMPUTE_ENGINES = {"pe": "tensor", "act": "scalar", "dve": "vector", "pool": "gpsimd"}
DMA_QUEUES = {"qs": "sync", "qp": "gpsimd", "qa": "scalar"}


DMA_SLOTS = {"qs": 10, "qp": 8, "qa": 6}


class Prog:
    def __init__(self, nc, stack):
        self.nc = nc
        self.stack = stack
        self.sems = {}
        self.cnt = {}
        for k in COMPUTE_ENGINES:
            self.sems[k] = stack.enter_context(nc.semaphore(f"sem_{k}"))
            self.cnt[k] = 0
        for q, n in DMA_SLOTS.items():
            for i in range(n):
                sk = f"{q}{i}"
                self.sems[sk] = stack.enter_context(nc.semaphore(f"sem_{sk}"))
                self.cnt[sk] = 0
        self.rr = {q: 0 for q in DMA_SLOTS}
        waiters = list(COMPUTE_ENGINES) + list(DMA_QUEUES)
        self.wm = {k: {} for k in waiters}
        self.writer = {}
        self.readers = {}
        self.sbufs = {}
        self.psums = {}
        self.sbuf_bytes = 0

    def sbuf(self, name, shape, dtype):
        t = self.sbufs.get(name)
        if t is None:
            t = self.nc.alloc_sbuf_tensor(name, shape, dtype)
            self.sbufs[name] = t
            self.sbuf_bytes += shape[1] * mybir.dt.size(dtype)
        return t

    def psum(self, name, shape, dtype=F32):
        t = self.psums.get(name)
        if t is None:
            t = self.nc.alloc_psum_tensor(name, shape, dtype)
            self.psums[name] = t
        return t

    @staticmethod
    def _ap(x):
        return x if isinstance(x, bass.AP) else x[:, :]

    def _keys(self, ap):
        h = ap.tensor
        name = getattr(h, "name", None) or str(id(h))
        if h in getattr(self.nc, "_psum_set", ()):
            pass
        bucket = None
        if name in self.psums:
            bucket = 2048
        elif name in ("A0", "A1", "AF0", "W16"):
            bucket = 256
        if bucket is not None:
            dsz = mybir.dt.size(ap.dtype)
            pstep = ap.ap[0][0] if ap.ap else 1
            col = (ap.offset % pstep) if pstep > 0 else ap.offset
            off = col * dsz
            span = 1
            for step, count in ap.ap[1:]:
                span += (count - 1) * abs(step)
            span *= dsz
            b0 = off // bucket
            b1 = (off + span - 1) // bucket
            return [("ps:" + name, b) for b in range(b0, b1 + 1)]
        return [("t:" + name, 0)]

    def _require(self, waiter, deps):
        for dk, dc in deps.items():
            if dc <= 0:
                continue
            if waiter == "pe" and dk == "pe":
                continue
            if self.wm[waiter].get(dk, 0) >= dc:
                continue
            getattr(self.nc, COMPUTE_ENGINES.get(waiter) or DMA_QUEUES[waiter]).wait_ge(
                self.sems[dk], dc)
            self.wm[waiter][dk] = dc

    def _collect(self, ins, outs):
        deps = {}
        for ap in ins:
            for k in self._keys(ap):
                w = self.writer.get(k)
                if w:
                    deps[w[0]] = max(deps.get(w[0], 0), w[1])
        for ap in outs:
            for k in self._keys(ap):
                w = self.writer.get(k)
                if w:
                    deps[w[0]] = max(deps.get(w[0], 0), w[1])
                for rk, rc in self.readers.get(k, {}).items():
                    deps[rk] = max(deps.get(rk, 0), rc)
        return deps

    def _commit(self, semkey, inc, ins, outs):
        c = self.cnt[semkey] + inc
        self.cnt[semkey] = c
        for ap in ins:
            for k in self._keys(ap):
                self.readers.setdefault(k, {})[semkey] = c
        for ap in outs:
            for k in self._keys(ap):
                self.writer[k] = (semkey, c)
                self.readers[k] = {}

    def _do(self, semkey, inc, ins, outs, emit):
        ins = [self._ap(x) for x in ins]
        outs = [self._ap(x) for x in outs]
        self._require(semkey, self._collect(ins, outs))
        instr = emit()
        instr.then_inc(self.sems[semkey], inc)
        self._commit(semkey, inc, ins, outs)
        return instr

    def dma(self, q, out, in_):
        eng = getattr(self.nc, DMA_QUEUES[q])
        out = self._ap(out)
        in_ = self._ap(in_)
        slot = self.rr[q] % DMA_SLOTS[q]
        self.rr[q] += 1
        sk = f"{q}{slot}"
        deps = self._collect([in_], [out])
        deps[sk] = max(deps.get(sk, 0), self.cnt[sk])
        self._require(q, deps)
        eng.dma_start(out, in_).then_inc(self.sems[sk], 16)
        self._commit(sk, 16, [in_], [out])

    def mm(self, out, lhsT, rhs, start=True, stop=True, tile_position=None):
        e = self.nc.tensor
        return self._do("pe", 1, [lhsT, rhs], [out],
                        lambda: e.matmul(self._ap(out), self._ap(lhsT), self._ap(rhs),
                                         start=start, stop=stop, tile_position=tile_position))

    def mm_group(self, out, pairs):
        e = self.nc.tensor
        ins = []
        for l, r in pairs:
            ins += [self._ap(l), self._ap(r)]
        outs = [self._ap(out)]
        self._require("pe", self._collect(ins, outs))
        n = len(pairs)
        for i, (l, r) in enumerate(pairs):
            instr = e.matmul(outs[0], self._ap(l), self._ap(r),
                             start=(i == 0), stop=(i == n - 1))
        instr.then_inc(self.sems["pe"], 1)
        self._commit("pe", 1, ins, outs)

    def transpose(self, out, in_, ident):
        e = self.nc.tensor
        return self._do("pe", 1, [in_, ident], [out],
                        lambda: e.transpose(self._ap(out), self._ap(in_), self._ap(ident)))

    def act(self, out, in_, func, bias=0.0, scale=1.0, accum_out=None):
        e = self.nc.scalar
        if not isinstance(bias, (int, float)):
            bias = self._ap(bias)
        if not isinstance(scale, (int, float)):
            scale = self._ap(scale)
        ins = [in_] + [x for x in (bias, scale) if isinstance(x, bass.AP)]
        outs = [out] + ([accum_out] if accum_out is not None else [])
        kw = dict(bias=bias, scale=scale)
        if accum_out is not None:
            kw["accum_out"] = self._ap(accum_out)
        return self._do("act", 1, ins, outs,
                        lambda: e.activation(self._ap(out), self._ap(in_), func, **kw))

    def _viface(self, engkey):
        return self.nc.vector if engkey == "dve" else self.nc.gpsimd

    def copy(self, engkey, out, in_):
        e = self._viface(engkey)
        return self._do(engkey, 1, [in_], [out], lambda: e.tensor_copy(self._ap(out), self._ap(in_)))

    def tt(self, engkey, out, in0, in1, op):
        e = self._viface(engkey)
        return self._do(engkey, 1, [in0, in1], [out],
                        lambda: e.tensor_tensor(self._ap(out), self._ap(in0), self._ap(in1), op=op))

    def ts(self, engkey, out, in0, s1, op0, s2=None, op1=None):
        e = self._viface(engkey)
        ins = [in0] + [x for x in (s1, s2) if isinstance(x, bass.AP)]
        kw = {}
        if op1 is not None:
            kw["op1"] = op1
        return self._do(engkey, 1, ins, [out],
                        lambda: e.tensor_scalar(self._ap(out), self._ap(in0),
                                                self._ap(s1) if isinstance(s1, bass.AP) else s1,
                                                self._ap(s2) if isinstance(s2, bass.AP) else s2,
                                                op0, **kw))

    def stt(self, engkey, out, in0, scalar, in1, op0, op1):
        e = self._viface(engkey)
        ins = [in0, in1] + ([scalar] if isinstance(scalar, bass.AP) else [])
        return self._do(engkey, 1, ins, [out],
                        lambda: e.scalar_tensor_tensor(
                            self._ap(out), self._ap(in0),
                            self._ap(scalar) if isinstance(scalar, bass.AP) else scalar,
                            self._ap(in1), op0=op0, op1=op1))

    def red(self, engkey, out, in_, op, axis=None, negate=False):
        e = self._viface(engkey)
        axis = axis if axis is not None else AX.X
        return self._do(engkey, 1, [in_], [out],
                        lambda: e.tensor_reduce(self._ap(out), self._ap(in_), axis=axis,
                                                op=op, negate=negate))

    def recip(self, out, in_):
        e = self.nc.vector
        return self._do("dve", 1, [in_], [out], lambda: e.reciprocal(self._ap(out), self._ap(in_)))

    def scan(self, out, d0, d1, initial, op0, op1):
        e = self.nc.vector
        return self._do("dve", 1, [d0, d1], [out],
                        lambda: e.tensor_tensor_scan(self._ap(out), self._ap(d0), self._ap(d1),
                                                     initial, op0=op0, op1=op1))

    def memset(self, out, val):
        e = self.nc.gpsimd
        return self._do("pool", 1, [], [out], lambda: e.memset(self._ap(out), val))

    def affine_sel(self, out, in_, compare_op, fill, base, pattern, channel_multiplier):
        e = self.nc.gpsimd
        return self._do("pool", 1, [in_], [out],
                        lambda: e.affine_select(out=self._ap(out), in_=self._ap(in_),
                                                compare_op=compare_op, fill=fill, base=base,
                                                pattern=pattern,
                                                channel_multiplier=channel_multiplier))

    def finish(self):
        for k, c in self.cnt.items():
            if c > 0:
                self.nc.sync.wait_ge(self.sems[k], c)

    def const(self, T, name):
        shape, dt = WEIGHT_SPECS[name]
        t = self.sbufs.get("K_" + name)
        if t is None:
            t = self.sbuf("K_" + name, shape, dt)
            self.dma("qs", t, T[name])
        return t

    def bank(self, i):
        t = self.psum("PSE", [128, 3072])
        return t[:, i * 512:(i + 1) * 512]

    def fbias(self, val, parts=128):
        key = f"fb_{val}"
        t = self.sbufs.get(key)
        if t is None:
            t = self.sbuf(key, [128, 1], F32)
            self.memset(t, val)
        return t[0:parts, 0:1]



def prep_inputs(inputs):
    f = np.ascontiguousarray
    d = {}
    d["wqT"] = f(inputs["L0_attn_W_Q"].transpose(2, 0, 1).reshape(1024, 1024))
    d["wkT"] = f(inputs["L0_attn_W_K"].transpose(2, 0, 1).reshape(1024, 1024))
    d["wvT"] = f(inputs["L0_attn_W_V"].transpose(2, 0, 1).reshape(1024, 1024))
    d["woT"] = f(inputs["L0_attn_W_O"].T)
    d["sim"] = f(inputs["L0_ffn_sim_matrix"])
    d["gates"] = f(inputs["L0_ffn_gates"].reshape(1, 16))
    qkv = np.concatenate([inputs["L0_ffn_q_proj"], inputs["L0_ffn_k_proj"],
                          inputs["L0_ffn_v_proj"]], axis=2)
    d["qkv_moe"] = f(qkv.transpose(1, 0, 2).reshape(1024, 16 * 192))
    d["o_stack"] = f(inputs["L0_ffn_o_proj"].reshape(1024, 1024))
    d["inprojT"] = f(inputs["L1_attn_in_proj_w"].T)
    d["conv_w"] = f(inputs["L1_attn_conv_w"])
    d["conv_b"] = f(inputs["L1_attn_conv_b"].reshape(2304, 1))
    d["dt_bias"] = f(inputs["L1_attn_dt_bias"].reshape(32, 1))
    d["A_log"] = f(inputs["L1_attn_A_log"].reshape(32, 1))
    d["D_rep"] = f(np.repeat(inputs["L1_attn_D"], 64).reshape(1, 2048))
    d["norm_w"] = f(inputs["L1_attn_norm_w"].reshape(1, 2048))
    d["outprojT"] = f(inputs["L1_attn_out_proj_w"].T)
    d["gateT"] = f(inputs["L1_attn_gate_w"].T)
    d["upT"] = f(inputs["L1_attn_up_w"].T)
    d["downT"] = f(inputs["L1_attn_down_w"].T)
    d["mixer_scale"] = f(inputs["L1_attn_mixer_scale"].reshape(1, 1024))
    d["mlp_scale"] = f(inputs["L1_attn_mlp_scale"].reshape(1, 1024))
    d["l1fwT"] = f(inputs["L1_ffn_weight"].T)
    d["disha"] = f(inputs["L1_ffn_disha"])
    d["c_ident"] = np.eye(128, dtype=np.float32)
    d["c_identb"] = np.eye(128).astype(ml_dtypes.bfloat16)
    d["c_ones"] = np.ones((128, 128), dtype=np.float32)
    d["c_onesb"] = np.ones((128, 128), dtype=ml_dtypes.bfloat16)
    tril = (np.arange(128)[:, None] <= np.arange(128)[None, :]).astype(np.float32)
    d["c_trilU"] = tril
    sel = np.zeros((128, 64), dtype=np.float32)
    sel[np.arange(128), np.arange(128) % 64] = 1.0
    d["c_selpool"] = sel.astype(ml_dtypes.bfloat16)
    mt = np.where(np.arange(128)[None, :] > np.arange(128)[:, None], -1e9, 0.0)
    d["c_maskT"] = mt.astype(np.float32)
    e127 = np.zeros((128, 1), dtype=np.float32)
    e127[127, 0] = 1.0
    d["c_e127"] = e127
    ii = np.zeros((64, 128), dtype=np.float32)
    ii[np.arange(128) % 64, np.arange(128)] = 1.0
    d["c_II"] = ii.astype(ml_dtypes.bfloat16)
    oh2 = np.zeros((16, 8, 128), dtype=np.float32)
    for pr in range(8):
        oh2[2 * pr, pr, 0:64] = 1.0
        oh2[2 * pr + 1, pr, 64:128] = 1.0
    d["c_oh2"] = oh2.reshape(16, 1024).astype(ml_dtypes.bfloat16)
    return d


WEIGHT_SPECS = {
    "wqT": ([1024, 1024], F32), "wkT": ([1024, 1024], F32), "wvT": ([1024, 1024], F32),
    "woT": ([1024, 1024], F32), "sim": ([1024, 16], F32), "gates": ([1, 16], F32),
    "qkv_moe": ([1024, 3072], F32), "o_stack": ([1024, 1024], F32),
    "inprojT": ([1024, 4384], F32), "conv_w": ([2304, 4], F32), "conv_b": ([2304, 1], F32),
    "dt_bias": ([32, 1], F32), "A_log": ([32, 1], F32), "D_rep": ([1, 2048], F32),
    "norm_w": ([1, 2048], F32), "outprojT": ([2048, 1024], F32),
    "gateT": ([1024, 4096], F32), "upT": ([1024, 4096], F32), "downT": ([4096, 1024], F32),
    "mixer_scale": ([1, 1024], F32), "mlp_scale": ([1, 1024], F32),
    "l1fwT": ([1024, 1024], F32), "disha": ([64, 1024], F32),
    "c_ident": ([128, 128], F32), "c_identb": ([128, 128], BF16),
    "c_ones": ([128, 128], F32), "c_onesb": ([128, 128], BF16),
    "c_trilU": ([128, 128], F32), "c_selpool": ([128, 64], BF16),
    "c_maskT": ([128, 128], F32), "c_II": ([64, 128], BF16), "c_oh2": ([16, 1024], BF16),
    "c_e127": ([128, 1], F32),
}

SCRATCH_SPECS = {
    "c_xnT": ([B_PER, 8, 128, S], BF16),
    "c_rw": ([B_PER, 16, 128, 16], F32),
    "c_rwT": ([B_PER, 16, S], BF16),
    "c_zflatT": ([B_PER, 8, 128, S], BF16),
    "c_amT": ([B_PER, 64, S], BF16),
    "c_s0T": ([B_PER, 8, 128, S], F32),
    "c_x1": ([B_PER, S, D], F32),
    "c_xn1": ([B_PER, S, D], F32),
    "c_xn1T": ([B_PER, 8, 128, S], BF16),
    "c_uT": ([B_PER, 8, 128, S], BF16),
    "c_zT": ([B_PER, 16, 128, S], BF16),
    "c_xbcT": ([B_PER, 18, 128, S], BF16),
    "c_dt": ([B_PER, 32, S], F32),
    "c_dtA": ([B_PER, 32, S], F32),
    "c_xp": ([B_PER, S, D], F32),
    "c_h2T": ([B_PER, 8, 128, S], BF16),
    "c_midT": ([32, 128, B_PER * S], BF16),
    "c_s1": ([B_PER, S, D], F32),
    "c_g1T": ([B_PER, 8, 128, S], F32),
}

PHASE_WRITES = {
    "p0": ["c_xnT", "c_rw", "c_rwT"],
    "p1": ["c_zflatT"],
    "p1b": ["c_s0T"],
    "p2": ["c_amT"],
    "p2b": ["c_x1"],
    "p3": ["c_xn1", "c_xn1T", "c_uT"],
    "p4": ["c_zT", "c_xbcT", "c_dt", "c_dtA"],
    "p5a": ["c_xp", "c_h2T"],
    "p5c": ["c_midT", "c_s1"],
    "p5d": ["c_g1T"],
    "p5e": [],
}
PHASE_READS = {
    "p0": [],
    "p1": ["c_xnT"],
    "p1b": ["c_zflatT"],
    "p2": ["c_xnT", "c_rw"],
    "p2b": ["c_rwT", "c_s0T", "c_amT"],
    "p3": ["c_x1"],
    "p4": ["c_uT"],
    "p5a": ["c_zT", "c_xbcT", "c_dt", "c_dtA", "c_xn1"],
    "p5c": ["c_h2T", "c_xp", "c_midT"],
    "p5d": ["c_xn1T"],
    "p5e": ["c_x1", "c_s1", "c_g1T"],
}
PHASE_NEEDS_X = {"p0", "p2b"}
PHASE_WRITES_OUT = {"p5e"}

ALL_PHASES = ["p0", "p1", "p1b", "p2", "p2b", "p3", "p4", "p5a", "p5c", "p5d", "p5e"]
GROUPS = [ALL_PHASES]



def stream_weight_b16(p, T, name, arena, off, n_in, n_out):
    nch = n_in // 128
    stg = p.sbuf("wstg", [128, 2 * 512], F32)
    src = T[name]
    i = 0
    for k in range(nch):
        for c0 in range(0, n_out, 512):
            cw = min(512, n_out - c0)
            half = (i % 2) * 512
            i += 1
            p.dma("qs", stg[:, half:half + cw], src[k * 128:(k + 1) * 128, c0:c0 + cw])
            dst = arena[:, off + k * n_out + c0: off + k * n_out + c0 + cw]
            if i % 2:
                p.copy("dve", dst, stg[:, half:half + cw])
            else:
                p.act(dst, stg[:, half:half + cw], AF.Identity)

    def chunk(k, c0, c1):
        return arena[:, off + k * n_out + c0: off + k * n_out + c1]
    return chunk



def emit_p0(p, T):
    ident = p.const(T, "c_ident")
    identb = p.const(T, "c_identb")
    onesf = p.const(T, "c_ones")

    simc = p.sbuf("simc", [128, 8 * 16], F32)
    for k in range(8):
        p.dma("qs", simc[:, k * 16:(k + 1) * 16], T["sim"][k * 128:(k + 1) * 128, :])
    ssq = p.sbuf("p0_ssq", [128, 16], F32)
    psg = p.psum("PSC", [128, 512], F32)
    pairs = []
    for k in range(8):
        p.act(ssq, simc[:, k * 16:(k + 1) * 16], AF.Square)
        p.mm(psg[0:1, 0:16], onesf[:, 0:1], ssq, start=(k == 0), stop=(k == 7))
    nsr = p.sbuf("p0_nsr", [1, 64], F32)
    p.act(nsr[0:1, 0:16], psg[0:1, 0:16], AF.Sqrt)
    p.ts("dve", nsr[0:1, 16:32], nsr[0:1, 0:16], 1e-12, ALU.max)
    p.recip(nsr[0:1, 32:48], nsr[0:1, 16:32])
    p.mm(psg[:, 16:32], onesf[0:1, :], nsr[0:1, 32:48])
    rnsB = p.sbuf("p0_rnsB", [128, 16], F32)
    p.act(rnsB, psg[:, 16:32], AF.Identity)
    simh = p.sbuf("p0_simh", [128, 8 * 16], F32)
    for k in range(8):
        p.tt("dve", simh[:, k * 16:(k + 1) * 16], simc[:, k * 16:(k + 1) * 16], rnsB, ALU.mult)
    grow = p.sbuf("p0_grow", [1, 16], F32)
    p.dma("qs", grow, T["gates"])
    gsig = p.sbuf("p0_gsig", [1, 16], F32)
    p.act(gsig, grow, AF.Sigmoid)
    p.mm(psg[:, 32:48], onesf[0:1, :], gsig)
    gB = p.sbuf("p0_gB", [128, 16], F32)
    p.act(gB, psg[:, 32:48], AF.Identity)

    xt = [p.sbuf(f"p0_x{i}", [128, 1024], F32) for i in range(2)]
    xaT = [p.sbuf(f"p0_xaT{i}", [128, 128], F32) for i in range(2)]
    xnTb = [p.sbuf(f"p0_xnTb{i}", [128, 128], BF16) for i in range(2)]
    psT = [p.bank(0), p.bank(1)]
    psL = p.bank(2)
    col = p.sbuf("p0_col", [128, 16], F32)
    xc = p.sbuf("p0_xc", [128, 1024], F32)
    sq = p.sbuf("p0_sq", [128, 1024], F32)
    xn = p.sbuf("p0_xn", [128, 1024], F32)
    ga = p.sbuf("p0_ga", [128, 16 * 8], F32)
    rwb = p.sbuf("p0_rwb", [128, 16], BF16)
    rwTb = p.sbuf("p0_rwTb", [16, 128], BF16)
    psB = p.psum("PSB", [128, 1024], BF16)

    col2 = p.sbuf("p0_ssq", [128, 16], F32)
    for s in range(B_PER):
        for t in range(16):
            cl = col if t % 2 == 0 else col2
            xb = xt[t % 2]
            p.dma("qs", xb, T["x"][s, t * 128:(t + 1) * 128, :])
            p.red("dve", cl[:, 0:1], xb, ALU.add)
            p.ts("dve", cl[:, 1:2], cl[:, 0:1], 1.0 / 1024, ALU.mult)
            p.ts("dve", xc, xb, cl[:, 1:2], ALU.subtract)
            p.act(sq, xc, AF.Square)
            p.red("dve", cl[:, 2:3], sq, ALU.add)
            p.act(cl[:, 3:4], cl[:, 2:3], AF.Sqrt, scale=1.0 / 1024, bias=p.fbias(1e-5))
            p.recip(cl[:, 4:5], cl[:, 3:4])
            p.ts("dve", xn, xc, cl[:, 4:5], ALU.mult)
            p.act(sq, xn, AF.Square)
            p.red("dve", cl[:, 5:6], sq, ALU.add)
            p.act(cl[:, 6:7], cl[:, 5:6], AF.Sqrt)
            p.ts("dve", cl[:, 7:8], cl[:, 6:7], 1e-12, ALU.max)
            p.recip(cl[:, 8:9], cl[:, 7:8])
            for k in range(8):
                ps = psT[k % 2]
                p.transpose(ps[:, 0:128], xn[:, k * 128:(k + 1) * 128], ident)
                xa = xaT[k % 2]
                p.act(xa, ps[:, 0:128], AF.Identity)
                p.act(xnTb[k % 2], ps[:, 0:128], AF.Identity)
                p.dma("qp", T["c_xnT"][s, k, :, t * 128:(t + 1) * 128], xnTb[k % 2])
                p.mm(psL[:, 0:16], xa, simh[:, k * 16:(k + 1) * 16],
                     start=(k == 0), stop=(k == 7))
            lg = ga[:, 0:16]; pos = ga[:, 16:32]; gated = ga[:, 32:48]
            l2 = ga[:, 48:64]; fb = ga[:, 64:80]; msk = ga[:, 80:96]
            ex = ga[:, 96:112]; rw = ga[:, 112:128]
            p.stt("dve", lg, psL[:, 0:16], cl[:, 8:9], gB, ALU.mult, ALU.subtract)
            p.red("dve", cl[:, 9:10], lg, ALU.max)
            p.act(gated, lg, AF.Relu)
            p.ts("dve", pos, lg, 0.0, ALU.is_gt)
            p.red("dve", cl[:, 10:11], pos, ALU.add)
            p.ts("dve", cl[:, 11:12], cl[:, 10:11], 0.5, ALU.is_lt)
            p.ts("dve", l2, lg, cl[:, 9:10], ALU.is_ge)
            p.stt("dve", l2, l2, -1e30, lg, ALU.mult, ALU.add)
            p.red("dve", cl[:, 12:13], l2, ALU.max)
            p.ts("dve", fb, lg, cl[:, 12:13], ALU.is_ge)
            p.stt("dve", msk, fb, cl[:, 11:12], pos, ALU.mult, ALU.add)
            p.act(ex, gated, AF.Exp)
            p.tt("dve", ex, ex, msk, ALU.mult)
            p.red("dve", cl[:, 13:14], ex, ALU.add)
            p.recip(cl[:, 14:15], cl[:, 13:14])
            p.ts("dve", rw, ex, cl[:, 14:15], ALU.mult)
            p.dma("qp", T["c_rw"][s, t, :, :], rw)
            p.copy("pool", rwb, rw)
            p.transpose(psB[0:16, 0:128], rwb[:, 0:16], identb)
            p.act(rwTb[0:16, :], psB[0:16, 0:128], AF.Identity)
            p.dma("qp", T["c_rwT"][s, :, t * 128:(t + 1) * 128], rwTb[0:16, :])




def ensure_masks(p):
    masks = p.sbufs.get("p1_masks")
    if masks is None:
        masks = p.sbuf("p1_masks", [128, 4 * 512], BF16)
        for i in range(4):
            sl = masks[:, i * 512:(i + 1) * 512]
            p.memset(sl, 0.0)
            p.affine_sel(sl, sl, ALU.is_ge, -1e9, -(i * 128), [[1, 512]], -1)
    return masks


def attn_core(p, T, qTa, kTa, v_of_kb, z_sink):
    onesb = p.const(T, "c_onesb")
    onesf = p.const(T, "c_ones")
    masks = ensure_masks(p)
    sqb = p.sbuf("p1_sqb", [128, 2048], BF16)
    rowf = p.sbuf("p1_rowf", [128, 2048], F32)
    cnb = sqb[0:1, 0:2048]
    mkc = p.sbuf("p1_mkc", [1, 2], F32)
    rcf = p.sbuf("p1_rcf", [128, 512], F32)
    rcB = p.sbuf("p1_rcB", [64, 512], F32)
    znb = p.sbuf("p1_znb", [64, 512], BF16)
    pT = [p.sbuf(f"p1_pT{i}", [128, 512], BF16) for i in range(2)]
    pT += [sqb[:, 0:512], sqb[:, 1024:1536]]
    PS = [p.bank(i) for i in range(6)]
    SC = [PS[0], PS[1], PS[4], PS[5]]
    p.act(sqb[0:64, :], qTa[0:64, :], AF.Square)
    for tb in range(4):
        p.mm(PS[3][0:1, 0:512], onesb[0:64, 0:1], sqb[0:64, tb * 512:(tb + 1) * 512])
        p.act(rowf[0:1, tb * 512:(tb + 1) * 512], PS[3][0:1, 0:512], AF.Sqrt)
    p.act(sqb[0:64, :], kTa[0:64, :], AF.Square)
    for tb in range(4):
        p.mm(PS[3][0:1, 0:512], onesb[0:64, 0:1], sqb[0:64, tb * 512:(tb + 1) * 512])
        p.act(rowf[64:65, tb * 512:(tb + 1) * 512], PS[3][0:1, 0:512], AF.Sqrt)
    p.red("dve", mkc[0:1, 0:1], rowf[64:65, :], ALU.max)
    p.ts("dve", mkc[0:1, 1:2], mkc[0:1, 0:1], -1.0, ALU.mult)
    p.ts("dve", rowf[0:1, :], rowf[0:1, :], mkc[0:1, 1:2], ALU.mult)
    p.copy("pool", cnb, rowf[0:1, :])
    for tb in range(4):
        p.mm(PS[3], onesb[0:1, 0:128], cnb[0:1, tb * 512:(tb + 1) * 512])
        p.act(qTa[64:65, tb * 512:(tb + 1) * 512], PS[3][64:65, 0:512], AF.Identity)
    for sb in range(4):
        kbmax = 4 * sb + 4

        def score(kb):
            ps = SC[kb % 4]
            p.mm(ps, kTa[0:65, kb * 128:(kb + 1) * 128],
                 qTa[0:65, sb * 512:(sb + 1) * 512])
            if kb >= 4 * sb:
                mi = kb - 4 * sb
                p.tt("dve", ps[:, 0:512], ps[:, 0:512],
                     masks[:, mi * 512:(mi + 1) * 512], ALU.add)
            p.act(pT[kb % 4], ps, AF.Exp)

        score(0)
        if kbmax > 1:
            score(1)
        for kb in range(kbmax):
            if kb + 2 < kbmax:
                score(kb + 2)
            p.mm(PS[2][0:65, :], v_of_kb(kb), pT[kb % 4],
                 start=(kb == 0), stop=(kb == kbmax - 1))
        p.recip(rcf[64:65, 0:512], PS[2][64:65, 0:512])
        p.mm(PS[3][0:64, 0:512], onesf[64:65, 0:64], rcf[64:65, 0:512])
        p.act(rcB, PS[3][0:64, 0:512], AF.Identity)
        p.tt("dve", znb, PS[2][0:64, :], rcB, ALU.mult)
        z_sink(sb, znb)



def emit_p1(p, T):
    identb = p.const(T, "c_identb")
    onesb = p.const(T, "c_onesb")
    onesf = p.const(T, "c_ones")
    W16 = p.sbuf("W16", [128, 24576], BF16)
    wq = stream_weight_b16(p, T, "wqT", W16, 0, 1024, 1024)
    wk = stream_weight_b16(p, T, "wkT", W16, 8192, 1024, 1024)
    wv = stream_weight_b16(p, T, "wvT", W16, 16384, 1024, 1024)
    xnT = p.sbuf("A0", [128, 16384], BF16)
    vv = p.sbuf("A1", [128, 16640], BF16)
    qTa = p.sbuf("p1_qT", [128, 2048], BF16)
    kTa = p.sbuf("p1_kT", [128, 2048], BF16)
    PS = [p.bank(i) for i in range(6)]

    for s in range(B_PER):
        for k in range(8):
            p.dma("qs", xnT[:, k * 2048:(k + 1) * 2048], T["c_xnT"][s, k])
        p.memset(vv[:, 0:16640].rearrange("p (t c) -> p t c", c=65)[:, :, 64:65], 1.0)
        for tt in range(16):
            for half in range(2):
                ps = PS[4 + half]
                p.mm_group(ps, [(xnT[:, k * 2048 + tt * 128: k * 2048 + (tt + 1) * 128],
                                 wv(k, half * 512, (half + 1) * 512)) for k in range(8)])
                src = ps[:, 0:512].rearrange("p (h c) -> p h c", h=8, c=64)
                dst_base = tt * 1040 + half * 8 * 65
                dst = vv[:, dst_base:dst_base + 8 * 65].rearrange(
                    "p (h c) -> p h c", h=8, c=65)[:, :, 0:64]
                p.act(dst, src, AF.Identity)
        for h in range(16):
            for tb in range(4):
                ps = PS[4 + (tb % 2)]
                p.mm_group(ps[0:64, :],
                           [(wq(k, h * 64, (h + 1) * 64),
                             xnT[:, k * 2048 + tb * 512: k * 2048 + (tb + 1) * 512])
                            for k in range(8)])
                p.act(qTa[0:64, tb * 512:(tb + 1) * 512], ps[0:64, :], AF.Identity,
                      scale=0.125)
            for tb in range(4):
                ps = PS[4 + (tb % 2)]
                p.mm_group(ps[0:64, :],
                           [(wk(k, h * 64, (h + 1) * 64),
                             xnT[:, k * 2048 + tb * 512: k * 2048 + (tb + 1) * 512])
                            for k in range(8)])
                p.act(kTa[0:64, tb * 512:(tb + 1) * 512], ps[0:64, :], AF.Identity)
            p.memset(kTa[64:65, :], 1.0)
            attn_core(p, T, qTa, kTa,
                      lambda kb: vv[:, kb * 1040 + h * 65: kb * 1040 + (h + 1) * 65],
                      lambda sb, znb_: p.dma(
                          "qp", T["c_zflatT"][s, h // 2, (h % 2) * 64:(h % 2) * 64 + 64,
                                              sb * 512:(sb + 1) * 512], znb_))


def emit_p1b(p, T):
    W16 = p.sbuf("W16", [128, 24576], BF16)
    wo = stream_weight_b16(p, T, "woT", W16, 0, 1024, 1024)
    zf = p.sbuf("A1", [128, 16640], BF16)
    swT = [p.sbuf(f"p1_swT{i}", [128, 512], F32) for i in range(2)]
    PS = [p.bank(i) for i in range(6)]
    for s in range(B_PER):
        for tb in range(4):
            half = (tb % 2) * 4096
            for k in range(8):
                p.dma("qs", zf[:, half + k * 512:half + (k + 1) * 512],
                      T["c_zflatT"][s, k, :, tb * 512:(tb + 1) * 512])
            for oc in range(8):
                ps = PS[4 + (oc % 2)]
                p.mm_group(ps, [(wo(k, oc * 128, (oc + 1) * 128),
                                 zf[:, half + k * 512:half + (k + 1) * 512])
                                for k in range(8)])
                p.act(swT[oc % 2], ps, AF.Identity)
                p.dma("qp", T["c_s0T"][s, oc, :, tb * 512:(tb + 1) * 512], swT[oc % 2])



def emit_p2(p, T):
    ident = p.const(T, "c_ident")
    W16 = p.sbuf("W16", [128, 24576], BF16)
    qkv = stream_weight_b16(p, T, "qkv_moe", W16, 0, 1024, 3072)
    xnT = p.sbuf("A0", [128, 16384], BF16)
    vm = p.sbuf("A1", [128, 16640], BF16)
    qmT = p.sbuf("p1_qT", [128, 2048], BF16)
    kmT = p.sbuf("p1_kT", [128, 2048], BF16)
    rwall = p.sbuf("p2_rwall", [128, 256], F32)
    acc = [p.sbuf(f"p2_acc{i}", [128, 192], F32) for i in range(2)]
    acc2 = [p.sbuf("p1_rcf", [128, 512], F32)[:, 0:192],
            p.sbuf("p1_rcf", [128, 512], F32)[:, 192:384]]
    PS = [p.bank(i) for i in range(6)]
    PSE = p.psum("PSE", [128, 3072])
    PSC = p.psum("PSC", [128, 512], F32)

    for s in range(B_PER):
        for k in range(8):
            p.dma("qs", xnT[:, k * 2048:(k + 1) * 2048], T["c_xnT"][s, k])
        for tt in range(16):
            p.dma("qp", rwall[:, tt * 16:(tt + 1) * 16], T["c_rw"][s, tt])
        p.memset(vm[:, 0:1040].rearrange("p (t c) -> p t c", c=65)[:, :, 64:65], 1.0)
        for tt in range(16):
            for k in range(8):
                for nb in range(6):
                    p.mm(PS[nb], xnT[:, k * 2048 + tt * 128: k * 2048 + (tt + 1) * 128],
                         qkv(k, nb * 512, (nb + 1) * 512), start=(k == 0), stop=(k == 7))
            a = acc[tt % 2]
            ab = acc2[tt % 2]
            p.ts("dve", a, PSE[:, 0:192], rwall[:, tt * 16:tt * 16 + 1], ALU.mult)
            p.ts("dve", ab, PSE[:, 192:384], rwall[:, tt * 16 + 1:tt * 16 + 2], ALU.mult)
            for e in range(2, 16, 2):
                p.stt("dve", a, PSE[:, e * 192:(e + 1) * 192],
                      rwall[:, tt * 16 + e:tt * 16 + e + 1], a, ALU.mult, ALU.add)
                p.stt("dve", ab, PSE[:, (e + 1) * 192:(e + 2) * 192],
                      rwall[:, tt * 16 + e + 1:tt * 16 + e + 2], ab, ALU.mult, ALU.add)
            p.tt("dve", a, a, ab, ALU.add)
            p.transpose(PSC[0:64, 0:128], a[:, 0:64], ident)
            p.act(qmT[0:64, tt * 128:(tt + 1) * 128], PSC[0:64, 0:128], AF.Identity,
                  scale=0.125)
            p.transpose(PSC[0:64, 128:256], a[:, 64:128], ident)
            p.act(kmT[0:64, tt * 128:(tt + 1) * 128], PSC[0:64, 128:256], AF.Identity)
            p.act(vm[:, tt * 65:tt * 65 + 64], a[:, 128:192], AF.Identity)
        p.memset(kmT[64:65, :], 1.0)
        attn_core(p, T, qmT, kmT, lambda kb: vm[:, kb * 65:(kb + 1) * 65],
                  lambda sb, znb_: p.dma("qp", T["c_amT"][s, :, sb * 512:(sb + 1) * 512],
                                         znb_))


def emit_p2b(p, T):
    ident = p.const(T, "c_ident")
    II = p.const(T, "c_II")
    _vm = p.sbuf("A1", [128, 16640], BF16)
    oh2 = _vm[0:16, 10240:11264]
    p.dma("qs", oh2, T["c_oh2"])
    W16 = p.sbuf("W16", [128, 24576], BF16)
    ost = stream_weight_b16(p, T, "o_stack", W16, 0, 1024, 1024)
    vm = p.sbuf("A1", [128, 16640], BF16)
    amT = vm[0:64, 8192:10240]
    rwTs = vm[0:16, 6144:8192]
    st2 = p.sbuf("p2_st2", [128, 512], BF16)
    rowf = p.sbuf("p1_rowf", [128, 2048], F32)
    g0c = rowf[:, 0:512]
    prodT = rowf[:, 512:1024]
    s0c = [rowf[:, 1024:1536], rowf[:, 1536:2048]]
    ptok = p.sbuf("AF0", [128, 4112], F32)
    xin = [p.sbuf(f"p0_x{i}", [128, 1024], F32) for i in range(2)]
    x1t = [p.sbuf("p0_xc", [128, 1024], F32), p.sbuf("p0_xn", [128, 1024], F32)]
    PS = [p.bank(i) for i in range(6)]

    for s in range(B_PER):
        p.dma("qp", amT, T["c_amT"][s])
        p.dma("qp", rwTs, T["c_rwT"][s])
        for tb in range(4):
            p.mm(PS[0], II[0:64, :], amT[0:64, tb * 512:(tb + 1) * 512])
            p.act(st2, PS[0], AF.Identity)
            for pr in range(8):
                p.mm(PS[1], oh2[0:16, pr * 128:(pr + 1) * 128],
                     rwTs[0:16, tb * 512:(tb + 1) * 512])
                p.tt("dve", vm[:, 2048 + pr * 512: 2048 + (pr + 1) * 512], PS[1], st2,
                     ALU.mult)
            for oc in range(8):
                ps = PS[2 + (oc % 2)]
                p.mm_group(ps, [(ost(k, oc * 128, (oc + 1) * 128),
                                 vm[:, 2048 + k * 512:2048 + (k + 1) * 512])
                                for k in range(8)])
                sc = s0c[oc % 2]
                p.dma("qs", sc, T["c_s0T"][s, oc, :, tb * 512:(tb + 1) * 512])
                p.act(g0c, ps, AF.Silu)
                p.tt("dve", prodT, g0c, sc, ALU.mult)
                for j in range(4):
                    p.transpose(PS[4 + (j % 2)][:, 0:128], prodT[:, j * 128:(j + 1) * 128],
                                ident)
                    p.act(ptok[:, j * 1024 + oc * 128: j * 1024 + (oc + 1) * 128],
                          PS[4 + (j % 2)][:, 0:128], AF.Identity)
            for j in range(4):
                xt_ = xin[j % 2]
                tg = tb * 4 + j
                p.dma("qs", xt_, T["x"][s, tg * 128:(tg + 1) * 128, :])
                p.tt("dve", x1t[j % 2], ptok[:, j * 1024:(j + 1) * 1024], xt_, ALU.add)
                p.dma("qp", T["c_x1"][s, tg * 128:(tg + 1) * 128, :], x1t[j % 2])



def emit_p3(p, T):
    identb = p.const(T, "c_identb")
    xt = [p.sbuf(f"p0_x{i}", [128, 1024], F32) for i in range(2)]
    col = p.sbuf("p0_col", [128, 16], F32)
    xc = p.sbuf("p0_xc", [128, 1024], F32)
    sq = p.sbuf("p0_sq", [128, 1024], F32)
    xn = p.sbuf("p0_xn", [128, 1024], F32)
    A1 = p.sbuf("A1", [128, 16640], BF16)
    xnb = A1[:, 0:1024]
    ub = A1[:, 4096:5120]
    oT = [p.sbuf(f"p0_xnTb{i}", [128, 128], BF16) for i in range(2)]
    PSB = p.psum("PSB", [128, 1024], BF16)
    eps32 = float(np.finfo(np.float32).eps)
    col2 = p.sbuf("p0_ssq", [128, 16], F32)
    for s in range(B_PER):
        for t in range(16):
            cl = col if t % 2 == 0 else col2
            xb = xt[t % 2]
            p.dma("qs", xb, T["c_x1"][s, t * 128:(t + 1) * 128, :])
            p.red("dve", cl[:, 0:1], xb, ALU.add)
            p.ts("dve", cl[:, 1:2], cl[:, 0:1], 1.0 / 1024, ALU.mult)
            p.ts("dve", xc, xb, cl[:, 1:2], ALU.subtract)
            p.act(sq, xc, AF.Square)
            p.red("dve", cl[:, 2:3], sq, ALU.add)
            p.act(cl[:, 3:4], cl[:, 2:3], AF.Sqrt, scale=1.0 / 1024, bias=p.fbias(1e-5))
            p.recip(cl[:, 4:5], cl[:, 3:4])
            p.ts("dve", xn, xc, cl[:, 4:5], ALU.mult)
            p.dma("qp", T["c_xn1"][s, t * 128:(t + 1) * 128, :], xn)
            p.act(sq, xn, AF.Square)
            p.red("dve", cl[:, 5:6], sq, ALU.add)
            p.act(cl[:, 6:7], cl[:, 5:6], AF.Sqrt, scale=1.0 / 1024, bias=p.fbias(eps32))
            p.recip(cl[:, 7:8], cl[:, 6:7])
            p.act(xnb, xn, AF.Identity)
            p.ts("dve", xc, xn, cl[:, 7:8], ALU.mult)
            p.act(ub, xc, AF.Identity)
            for k in range(8):
                p.transpose(PSB[:, 0:128], xnb[:, k * 128:(k + 1) * 128], identb)
                p.act(oT[0], PSB[:, 0:128], AF.Identity)
                p.dma("qp", T["c_xn1T"][s, k, :, t * 128:(t + 1) * 128], oT[0])
                p.transpose(PSB[:, 128:256], ub[:, k * 128:(k + 1) * 128], identb)
                p.act(oT[1], PSB[:, 128:256], AF.Identity)
                p.dma("qa", T["c_uT"][s, k, :, t * 128:(t + 1) * 128], oT[1])



def emit_p4(p, T):
    uT = p.sbuf("A0", [128, 16384], BF16)
    A1 = p.sbuf("A1", [128, 16640], BF16)
    xpad = p.sbuf("AF0", [128, 4112], F32)
    wstg = xpad[:, 2064:4112]
    wbuf = [A1[:, 4096 + i * 1024:4096 + (i + 1) * 1024] for i in range(4)]
    convw = p.sbuf("p4_cw", [128, 72], F32)
    convb = p.sbuf("p4_cb", [128, 18], F32)
    zxb = A1[:, 0:2048]
    xbc_pad = xpad[:, 0:2051]
    convo = p.sbuf("p5_ytok", [128, 2048], F32)
    dtr = p.sbuf("p1_rowf", [128, 2048], F32)
    dts = convo
    dtb_col = p.sbuf("p4_dtb", [32, 1], F32)
    alog = p.sbuf("p4_alog", [32, 2], F32)
    PS = [p.bank(i) for i in range(6)]
    p.dma("qs", dtb_col, T["dt_bias"])
    p.dma("qs", alog[:, 0:1], T["A_log"])
    p.act(alog[0:32, 1:2], alog[0:32, 0:1], AF.Exp)
    p.ts("dve", alog[0:32, 1:2], alog[0:32, 1:2], -1.0, ALU.mult)
    for cc in range(18):
        p.dma("qs", convw[:, cc * 4:(cc + 1) * 4], T["conv_w"][cc * 128:(cc + 1) * 128, :])
        p.dma("qs", convb[:, cc:cc + 1], T["conv_b"][cc * 128:(cc + 1) * 128, :])
    for s in range(B_PER):
        for k in range(8):
            p.dma("qs", uT[:, k * 2048:(k + 1) * 2048], T["c_uT"][s, k])
        for oc in range(35):
            wcols = 128 if oc < 34 else 32
            w = wbuf[oc % 4]
            half = (oc % 2) * 1024
            for k in range(8):
                p.dma("qs", wstg[:, half + k * 128:half + k * 128 + wcols],
                      T["inprojT"][k * 128:(k + 1) * 128, oc * 128:oc * 128 + wcols])
            if wcols == 128:
                if oc % 2:
                    p.copy("dve", w[:, 0:1024], wstg[:, half:half + 1024])
                else:
                    p.act(w[:, 0:1024], wstg[:, half:half + 1024], AF.Identity)
            else:
                p.copy("pool", w[:, 0:256].rearrange("p (k c) -> p k c", k=8),
                       wstg[:, half:half + 1024].rearrange("p (k c) -> p k c", k=8)[:, :, 0:32])
            for tb in range(4):
                ps = PS[4 + (tb % 2)]
                p.mm_group(ps[0:wcols, 0:512],
                           [(w[:, k * wcols:(k + 1) * wcols],
                             uT[:, k * 2048 + tb * 512:k * 2048 + (tb + 1) * 512])
                            for k in range(8)])
                if oc < 16:
                    p.act(zxb[:, tb * 512:(tb + 1) * 512], ps[:, 0:512], AF.Identity)
                elif oc < 34:
                    p.act(xbc_pad[:, 3 + tb * 512:3 + (tb + 1) * 512], ps[:, 0:512],
                          AF.Identity)
                else:
                    p.act(dtr[0:32, tb * 512:(tb + 1) * 512], ps[0:32, 0:512],
                          AF.Identity)
            if oc < 16:
                p.dma("qp", T["c_zT"][s, oc], zxb)
            elif oc < 34:
                cc = oc - 16
                p.memset(xbc_pad[:, 0:3], 0.0)
                p.ts("dve", convo, xbc_pad[:, 0:2048], convw[:, cc * 4:cc * 4 + 1],
                     ALU.mult)
                for j in range(1, 4):
                    p.stt("dve", convo, xbc_pad[:, j:j + 2048],
                          convw[:, cc * 4 + j:cc * 4 + j + 1], convo, ALU.mult, ALU.add)
                p.act(zxb, convo, AF.Silu, bias=convb[:, cc:cc + 1])
                p.dma("qp", T["c_xbcT"][s, cc], zxb)
        p.act(dts[0:32, :], dtr[0:32, :], AF.Abs, bias=dtb_col)
        p.act(dts[0:32, :], dts[0:32, :], AF.Exp, scale=-1.0)
        p.act(dts[0:32, :], dts[0:32, :], AF.Ln, bias=p.fbias(1.0, 32))
        p.act(dtr[0:32, :], dtr[0:32, :], AF.Relu, bias=dtb_col)
        p.tt("dve", dtr[0:32, :], dtr[0:32, :], dts[0:32, :], ALU.add)
        p.dma("qp", T["c_dt"][s], dtr[0:32, :])
        p.ts("dve", dts[0:32, :], dtr[0:32, :], alog[0:32, 1:2], ALU.mult)
        p.dma("qp", T["c_dtA"][s], dts[0:32, :])



def emit_p5a(p, T):
    ident = p.const(T, "c_ident")
    identb = p.const(T, "c_identb")
    onesf = p.const(T, "c_ones")
    trilU = p.const(T, "c_trilU")
    maskT = p.const(T, "c_maskT")
    e127 = p.const(T, "c_e127")
    W16 = p.sbuf("W16", [128, 24576], BF16)
    opw = stream_weight_b16(p, T, "outprojT", W16, 0, 2048, 1024)
    rows = p.sbuf("p1_rowf", [128, 2048], F32)[0:1, 0:2048]
    nwB = [p.sbuf("p1_kT", [128, 2048], BF16)[:, :].bitcast(F32),
           p.sbuf("p1_sqb", [128, 2048], BF16)[:, :].bitcast(F32)]
    msB = p.sbuf("p1_qT", [128, 2048], BF16)[:, :].bitcast(F32)
    dB = p.sbuf("p5_dB", [128, 64], F32)
    PS = [p.bank(i) for i in range(6)]
    PSC = p.psum("PSC", [128, 512], F32)
    PSB = p.psum("PSB", [128, 1024], BF16)
    p.dma("qs", rows, T["norm_w"])
    for i in range(4):
        p.mm(PS[4], onesf[0:1, 0:128], rows[0:1, i * 512:(i + 1) * 512])
        p.act(nwB[i // 2][:, (i % 2) * 512:(i % 2 + 1) * 512], PS[4], AF.Identity)
    p.dma("qs", rows, T["D_rep"])
    p.mm(PS[4][:, 0:32], onesf[0:1, 0:128],
         rows[0:1, 0:2048].rearrange("o (h j) -> o h j", j=64)[:, :, 0:1])
    p.act(dB[:, 0:32], PS[4][:, 0:32], AF.Identity)
    p.dma("qs", rows[0:1, 0:1024], T["mixer_scale"])
    for i in range(2):
        p.mm(PS[4], onesf[0:1, 0:128], rows[0:1, i * 512:(i + 1) * 512])
        p.act(msB[:, i * 512:(i + 1) * 512], PS[4], AF.Identity)

    A0 = p.sbuf("A0", [128, 16384], BF16)
    xt_fm = A0[:, 0:2048]
    z_fm = A0[:, 2048:4096]
    BT = A0[:, 4096:4224]
    CT = A0[:, 4224:4352]
    Btok = A0[:, 4352:4480]
    X0 = A0[:, 4608:6656]
    X1 = A0[:, 6656:8704]
    X2 = A0[:, 8704:10752]
    gmt = [A0[:, 10752:10880], A0[:, 11136:11264]]
    gtm = A0[:, 10880:11008]
    eh = [A0[:, 11008:11136], A0[:, 11264:11392]]
    rbt2 = [p.sbuf("p2_st2", [128, 512], BF16)[:, :].bitcast(F32)[:, 0:128],
            A0[:, 11392:11648].bitcast(F32)]
    ynT = A0[:, 12288:14336]
    ynb = A0[:, 14336:16384]
    hT = p.sbuf("p1_rowf", [128, 2048], F32)
    hT16 = p.sbuf("p1_masks", [128, 4 * 512], BF16)
    ytok = p.sbuf("p5_ytok", [128, 2048], F32)
    zsil = [p.sbuf("p1_rcf", [128, 512], F32), p.sbuf("p1_swT0", [128, 512], F32)]
    sqa = p.sbuf("AF0", [128, 4112], F32)
    _rw = p.sbuf("p2_rwall", [128, 256], F32)
    dtAT = _rw[:, 0:32]
    dtT = _rw[:, 32:64]
    cum = _rw[:, 64:96]
    cumn = _rw[:, 96:128]
    e1 = _rw[:, 128:160]
    e2 = _rw[:, 160:192]
    lamB = _rw[:, 192:224]
    _rcB = p.sbuf("p1_rcB", [64, 512], F32)
    cq = _rcB[0:1, 256:320]
    dtc = _rcB[0:32, 0:256]
    _st2 = p.sbuf("p2_st2", [128, 512], BF16)[:, :].bitcast(F32)
    tmp64 = [_st2[:, 128:192], _st2[:, 192:256]]
    xp_t = p.sbuf("p0_xc", [128, 1024], F32)
    xn1t = p.sbuf("p0_x0", [128, 1024], F32)
    h2 = p.sbuf("p0_xn", [128, 1024], F32)
    h2b = A0[:, 14336:15360]
    col = p.sbuf("p0_col", [128, 16], F32)
    eps32 = float(np.finfo(np.float32).eps)

    def bc(t, n):
        return t[:, :].rearrange("p (h one) -> p h one", one=1).broadcast_to([128, 32, n])

    for s in range(B_PER):
        p.memset(hT, 0.0)
        p.memset(hT16, 0.0)
        for c in range(NCHUNK):
            cs = slice(c * 128, (c + 1) * 128)
            xsrc = T["c_xbcT"].rearrange("s k p t -> s p k t")
            zsrc = T["c_zT"].rearrange("s k p t -> s p k t")
            for k4 in range(4):
                p.dma("qs", xt_fm[:, k4 * 512:(k4 + 1) * 512].rearrange(
                    "p (k t) -> p k t", k=4),
                    xsrc[s, :, k4 * 4:(k4 + 1) * 4, cs])
                p.dma("qa", z_fm[:, k4 * 512:(k4 + 1) * 512].rearrange(
                    "p (k t) -> p k t", k=4),
                    zsrc[s, :, k4 * 4:(k4 + 1) * 4, cs])
            p.dma("qs", BT, T["c_xbcT"][s, 16, :, cs])
            p.dma("qs", CT, T["c_xbcT"][s, 17, :, cs])
            p.dma("qs", dtc[0:32, 0:128], T["c_dt"][s, :, cs])
            p.dma("qs", dtc[0:32, 128:256], T["c_dtA"][s, :, cs])
            p.transpose(PSC[:, 0:32], dtc[0:32, 128:256], ident[0:32, 0:32])
            p.act(dtAT[:, 0:32], PSC[:, 0:32], AF.Identity)
            p.transpose(PSC[:, 128:160], dtc[0:32, 0:128], ident[0:32, 0:32])
            p.act(dtT[:, 0:32], PSC[:, 128:160], AF.Identity)
            p.mm(PSC[:, 256:288], trilU, dtAT[:, 0:32])
            p.act(cum, PSC[:, 256:288], AF.Identity)
            p.act(cumn, PSC[:, 256:288], AF.Identity, scale=-1.0)
            p.act(e1, PSC[:, 256:288], AF.Exp)
            p.mm(PSC[0:1, 288:320], e127, cum)
            p.act(cq[0:1, 0:32], PSC[0:1, 288:320], AF.Identity)
            p.mm(PSC[:, 320:352], onesf[0:1, 0:128], cq[0:1, 0:32])
            p.tt("dve", e2, PSC[:, 320:352], cum, ALU.subtract)
            p.act(e2, e2, AF.Exp)
            p.act(cq[0:1, 32:64], cq[0:1, 0:32], AF.Exp)
            p.mm(PSC[:, 352:384], onesf[0:1, 0:128], cq[0:1, 32:64])
            p.act(lamB, PSC[:, 352:384], AF.Identity)
            for r in range(2):
                for k in range(8):
                    p.transpose(PSB[:, k * 128:(k + 1) * 128],
                                xt_fm[:, (r * 8 + k) * 128:(r * 8 + k + 1) * 128], identb)
                p.act(X0[:, r * 1024:(r + 1) * 1024], PSB[:, :], AF.Identity)
            p.tt("pool", X1.rearrange("p (h j) -> p h j", h=32),
                 X0.rearrange("p (h j) -> p h j", h=32), bc(dtT, 64), ALU.mult)
            p.tt("pool", X2.rearrange("p (h j) -> p h j", h=32),
                 X1.rearrange("p (h j) -> p h j", h=32), bc(e2, 64), ALU.mult)
            p.transpose(PSB[:, 0:128], BT, identb)
            p.act(Btok, PSB[:, 0:128], AF.Identity)
            p.mm(PS[4][:, 0:128], BT, CT)
            p.tt("dve", gtm, PS[4][:, 0:128], trilU, ALU.mult)
            if c == 0:
                p.memset(ytok, 0.0)
            else:
                for qd in range(4):
                    p.mm(PS[qd], CT, hT16[:, qd * 512:(qd + 1) * 512])
                    p.tt("dve", ytok[:, qd * 512:(qd + 1) * 512].rearrange(
                        "p (h j) -> p h j", h=8),
                        PS[qd][:, 0:512].rearrange("p (h j) -> p h j", h=8),
                        bc(e1, 64)[:, qd * 8:(qd + 1) * 8, :], ALU.mult)
            for h in range(NHEADS):
                i2 = h % 2
                p.act(rbt2[i2], maskT, AF.Identity, bias=cum[:, h:h + 1])
                p.transpose(PSC[:, i2 * 128:(i2 + 1) * 128], rbt2[i2], ident)
                p.act(eh[i2], PSC[:, i2 * 128:(i2 + 1) * 128], AF.Exp,
                      bias=cumn[:, h:h + 1])
                p.tt("pool", gmt[i2], gtm, eh[i2], ALU.mult)
                p.mm(PS[5][0:128, i2 * 64:(i2 + 1) * 64], gmt[i2],
                     X1[:, h * 64:(h + 1) * 64])
                p.stt("dve", tmp64[i2], X0[:, h * 64:(h + 1) * 64], dB[:, h:h + 1],
                      PS[5][:, i2 * 64:(i2 + 1) * 64], ALU.mult, ALU.add)
                p.tt("pool", ytok[:, h * 64:(h + 1) * 64], ytok[:, h * 64:(h + 1) * 64],
                     tmp64[i2], ALU.add)
            for qd in range(4):
                p.mm(PS[qd], Btok, X2[:, qd * 512:(qd + 1) * 512])
            for qd in range(4):
                p.tt("dve", hT[:, qd * 512:(qd + 1) * 512].rearrange(
                    "p (h j) -> p h j", h=8),
                    hT[:, qd * 512:(qd + 1) * 512].rearrange("p (h j) -> p h j", h=8),
                    bc(lamB, 64)[:, qd * 8:(qd + 1) * 8, :], ALU.mult)
                p.tt("dve", hT[:, qd * 512:(qd + 1) * 512], hT[:, qd * 512:(qd + 1) * 512],
                     PS[qd][:, 0:512], ALU.add)
            p.act(hT16, hT, AF.Identity)
            for r in range(2):
                for k in range(8):
                    p.transpose(PSB[:, k * 128:(k + 1) * 128],
                                z_fm[:, (r * 8 + k) * 128:(r * 8 + k + 1) * 128], identb)
                for hlf in range(2):
                    zs = zsil[hlf]
                    p.act(zs, PSB[:, hlf * 512:(hlf + 1) * 512], AF.Silu)
                    p.tt("pool", ytok[:, r * 1024 + hlf * 512:r * 1024 + (hlf + 1) * 512],
                         ytok[:, r * 1024 + hlf * 512:r * 1024 + (hlf + 1) * 512],
                         zs, ALU.mult)
            p.act(sqa[:, 0:2048], ytok, AF.Square)
            p.red("dve", col[:, 0:1], sqa[:, 0:2048], ALU.add)
            p.act(col[:, 1:2], col[:, 0:1], AF.Sqrt, scale=1.0 / 2048, bias=p.fbias(1e-5))
            p.recip(col[:, 2:3], col[:, 1:2])
            p.ts("dve", ytok, ytok, col[:, 2:3], ALU.mult)
            p.tt("pool", ytok[:, 0:1024], ytok[:, 0:1024], nwB[0], ALU.mult)
            p.tt("pool", ytok[:, 1024:2048], ytok[:, 1024:2048], nwB[1], ALU.mult)
            p.act(ynb, ytok, AF.Identity)
            for r in range(2):
                for k in range(8):
                    p.transpose(PSB[:, k * 128:(k + 1) * 128],
                                ynb[:, (r * 8 + k) * 128:(r * 8 + k + 1) * 128], identb)
                p.act(ynT[:, r * 1024:(r + 1) * 1024], PSB[:, :], AF.Identity)
            for half in range(2):
                p.mm_group(PS[4 + half],
                           [(ynT[:, d * 128:(d + 1) * 128],
                             opw(d, half * 512, (half + 1) * 512)) for d in range(16)])
            p.dma("qs", xn1t, T["c_xn1"][s, cs, :])
            for half in range(2):
                hs = slice(half * 512, (half + 1) * 512)
                p.tt("dve", xp_t[:, hs], PS[4 + half][:, 0:512], msB[:, hs], ALU.mult)
                p.tt("dve", xp_t[:, hs], xp_t[:, hs], xn1t[:, hs], ALU.add)
            p.dma("qs", T["c_xp"][s, cs, :], xp_t)
            p.act(sqa[:, 2048:3072], xp_t, AF.Square)
            p.red("dve", col[:, 3:4], sqa[:, 2048:3072], ALU.add)
            p.act(col[:, 4:5], col[:, 3:4], AF.Sqrt, scale=1.0 / 1024, bias=p.fbias(eps32))
            p.recip(col[:, 5:6], col[:, 4:5])
            p.ts("dve", h2, xp_t, col[:, 5:6], ALU.mult)
            p.act(h2b, h2, AF.Identity)
            for k in range(8):
                p.transpose(PSB[:, (k % 2) * 128:(k % 2) * 128 + 128],
                            h2b[:, k * 128:(k + 1) * 128], identb)
                p.act(gmt[k % 2], PSB[:, (k % 2) * 128:(k % 2) * 128 + 128], AF.Identity)
                p.dma("qs", T["c_h2T"][s, k, :, cs], gmt[k % 2])



def emit_p5c(p, T):
    onesf = p.const(T, "c_ones")
    W16 = p.sbuf("W16", [128, 24576], BF16)
    h2T = p.sbuf("A0", [128, 16384], BF16)
    A1 = p.sbuf("A1", [128, 16640], BF16)
    wstg = p.sbuf("AF0", [128, 4112], F32)[:, 2064:4112]
    wg = [A1[:, 8192 + i * 1024:8192 + (i + 1) * 1024] for i in range(4)]
    wu = [A1[:, 12288 + i * 1024:12288 + (i + 1) * 1024] for i in range(4)]
    silg = p.sbuf("p1_rowf", [128, 2048], F32)
    midc = [p.sbuf("p2_st2", [128, 512], BF16), p.sbuf("p1_pT0", [128, 512], BF16)]
    midbuf = [A1[:, i * 4096:(i + 1) * 4096] for i in range(2)]
    mlsrow = p.sbuf("p1_rowf", [128, 2048], F32)[0:1, 0:1024]
    mlsB = p.sbuf("p1_qT", [128, 2048], BF16)[:, :].bitcast(F32)
    xpt = p.sbuf("p0_x0", [128, 1024], F32)
    s1t = p.sbuf("p0_xc", [128, 1024], F32)
    PS = [p.bank(i) for i in range(6)]
    p.dma("qs", mlsrow, T["mlp_scale"])
    for i in range(2):
        p.mm(PS[4], onesf[0:1, 0:128], mlsrow[0:1, i * 512:(i + 1) * 512])
        p.act(mlsB[:, i * 512:(i + 1) * 512], PS[4], AF.Identity)
    for s in range(B_PER):
        for k in range(8):
            p.dma("qs", h2T[:, k * 2048:(k + 1) * 2048], T["c_h2T"][s, k])
        for m in range(32):
            g = wg[m % 4]
            u = wu[m % 4]
            half = (m % 2) * 1024
            for k in range(8):
                p.dma("qs", wstg[:, half + k * 128:half + (k + 1) * 128],
                      T["gateT"][k * 128:(k + 1) * 128, m * 128:(m + 1) * 128])
            p.act(g, wstg[:, half:half + 1024], AF.Identity)
            for k in range(8):
                p.dma("qa", wstg[:, half + k * 128:half + (k + 1) * 128],
                      T["upT"][k * 128:(k + 1) * 128, m * 128:(m + 1) * 128])
            p.copy("dve", u, wstg[:, half:half + 1024])
            for tb in range(4):
                pg = PS[4] if tb % 2 == 0 else PS[2]
                pu = PS[5] if tb % 2 == 0 else PS[3]
                p.mm_group(pg, [(g[:, k * 128:(k + 1) * 128],
                                 h2T[:, k * 2048 + tb * 512:k * 2048 + (tb + 1) * 512])
                                for k in range(8)])
                p.mm_group(pu, [(u[:, k * 128:(k + 1) * 128],
                                 h2T[:, k * 2048 + tb * 512:k * 2048 + (tb + 1) * 512])
                                for k in range(8)])
                p.act(silg[:, (tb % 2) * 512:(tb % 2) * 512 + 512], pg, AF.Silu)
                p.tt("dve", midc[tb % 2], silg[:, (tb % 2) * 512:(tb % 2) * 512 + 512],
                     pu, ALU.mult)
                p.dma("qp", T["c_midT"][m, :, s * 2048 + tb * 512:s * 2048 + (tb + 1) * 512],
                      midc[tb % 2])
    for half in range(2):
        dw = stream_weight_b16(p, T, "downT", W16, 0, 4096, 512) if half == 0 else None
        if half == 1:
            dw = _stream_down_half(p, T, W16)
        for s in range(B_PER):
            for tq in range(16):
                mb = midbuf[tq % 2]
                src = T["c_midT"].rearrange("d p t -> p d t")
                for g8 in range(8):
                    p.dma("qs", mb[:, g8 * 512:(g8 + 1) * 512].rearrange(
                        "p (d t) -> p d t", d=4),
                        src[:, g8 * 4:(g8 + 1) * 4,
                            s * 2048 + tq * 128:s * 2048 + (tq + 1) * 128])
                ps = PS[tq % 2]
                p.mm_group(ps, [(mb[:, d * 128:(d + 1) * 128], dw(d, 0, 512))
                                for d in range(32)])
                hs = slice(half * 512, (half + 1) * 512)
                p.dma("qs", xpt[:, hs], T["c_xp"][s, tq * 128:(tq + 1) * 128, hs])
                p.tt("dve", s1t[:, hs], ps[:, 0:512], mlsB[:, hs], ALU.mult)
                p.tt("dve", s1t[:, hs], s1t[:, hs], xpt[:, hs], ALU.add)
                p.dma("qp", T["c_s1"][s, tq * 128:(tq + 1) * 128, hs], s1t[:, hs])


def _stream_down_half(p, T, W16):
    stg = p.sbuf("wstg", [128, 2 * 512], F32)
    src = T["downT"]
    for k in range(32):
        half = (k % 2) * 512
        p.dma("qs", stg[:, half:half + 512], src[k * 128:(k + 1) * 128, 512:1024])
        p.copy("pool", W16[:, k * 512:(k + 1) * 512], stg[:, half:half + 512])

    def chunk(k, c0, c1):
        return W16[:, k * 512:(k + 1) * 512]
    return chunk



def emit_p5d(p, T):
    selp = p.const(T, "c_selpool")
    W16 = p.sbuf("W16", [128, 24576], BF16)
    fw = stream_weight_b16(p, T, "l1fwT", W16, 0, 1024, 1024)
    dstg = p.sbuf("AF0", [128, 4112], F32)[:, 2064:4112]
    dish = p.sbuf("A1", [128, 16640], BF16)[0:64, 2048:3072]
    p.dma("qs", dstg[0:64, 0:1024], T["disha"])
    p.copy("pool", dish, dstg[0:64, 0:1024])
    xn1T = p.sbuf("A0", [128, 16384], BF16)
    A1 = p.sbuf("A1", [128, 16640], BF16)
    pooledT = A1[0:64, 0:2048]
    g1c = p.sbuf("p1_rowf", [128, 2048], F32)
    PS = [p.bank(i) for i in range(6)]
    for s in range(B_PER):
        for k in range(8):
            p.dma("qs", xn1T[:, k * 2048:(k + 1) * 2048], T["c_xn1T"][s, k])
        for tb in range(4):
            p.mm_group(PS[4][0:64, 0:512],
                       [(selp[:, 0:64], xn1T[:, k * 2048 + tb * 512:k * 2048 + (tb + 1) * 512])
                        for k in range(8)])
            p.act(pooledT[0:64, tb * 512:(tb + 1) * 512], PS[4][0:64, 0:512], AF.Identity)
        for tb in range(4):
            for oc in range(8):
                ps = PS[oc % 2]
                pairs = [(fw(k, oc * 128, (oc + 1) * 128),
                          xn1T[:, k * 2048 + tb * 512:k * 2048 + (tb + 1) * 512])
                         for k in range(8)]
                pairs.append((dish[0:64, oc * 128:(oc + 1) * 128],
                              pooledT[0:64, tb * 512:(tb + 1) * 512]))
                p.mm_group(ps, pairs)
                p.act(g1c[:, (oc % 4) * 512:(oc % 4 + 1) * 512], ps, AF.Silu)
                p.dma("qp", T["c_g1T"][s, oc, :, tb * 512:(tb + 1) * 512],
                      g1c[:, (oc % 4) * 512:(oc % 4 + 1) * 512])



def emit_p5e(p, T):
    ident = p.const(T, "c_ident")
    gf = [p.sbuf(f"p5e_gf{i}", [128, 128], F32) for i in range(2)]
    gtok = p.sbuf("AF0", [128, 4112], F32)
    x1t = [p.sbuf(f"p0_x{i}", [128, 1024], F32) for i in range(2)]
    s1t = [p.sbuf("p0_xc", [128, 1024], F32), p.sbuf("p0_xn", [128, 1024], F32)]
    ot = p.sbuf("p0_sq", [128, 1024], F32)
    PSC = p.psum("PSC", [128, 512], F32)
    for s in range(B_PER):
        for tq in range(16):
            gslot = (tq % 4) * 1024
            for k in range(8):
                p.dma("qs", gf[k % 2], T["c_g1T"][s, k, :, tq * 128:(tq + 1) * 128])
                p.transpose(PSC[:, (k % 2) * 128:(k % 2) * 128 + 128], gf[k % 2], ident)
                p.act(gtok[:, gslot + k * 128:gslot + (k + 1) * 128],
                      PSC[:, (k % 2) * 128:(k % 2) * 128 + 128], AF.Identity)
            p.dma("qp", x1t[tq % 2], T["c_x1"][s, tq * 128:(tq + 1) * 128, :])
            p.dma("qa", s1t[tq % 2], T["c_s1"][s, tq * 128:(tq + 1) * 128, :])
            p.tt("dve", ot, s1t[tq % 2], gtok[:, gslot:gslot + 1024], ALU.mult)
            p.tt("dve", ot, ot, x1t[tq % 2], ALU.add)
            p.dma("qs", T["y"][s, tq * 128:(tq + 1) * 128, :], ot)


EMITTERS = {"p0": emit_p0, "p1": emit_p1, "p1b": emit_p1b,
            "p2": emit_p2, "p2b": emit_p2b, "p3": emit_p3, "p4": emit_p4,
            "p5a": emit_p5a, "p5c": emit_p5c, "p5d": emit_p5d, "p5e": emit_p5e}


def _scratch_kind(name, phases, keep=()):
    w = any(name in PHASE_WRITES[ph] for ph in phases)
    r = any(name in PHASE_READS[ph] for ph in phases)
    if w:
        consumed_outside = any(name in PHASE_READS[ph] for ph in PHASE_READS
                               if ph not in phases)
        if consumed_outside or name in keep:
            return "ExternalOutput"
        return "Internal"
    if r:
        return "ExternalInput"
    return None


def build_launch(phases, keep=()):
    nc = bass.Bass("TRN2", target_bir_lowering=False, debug=False)
    stack = ExitStack()
    T = {}
    needs_x = any(ph in PHASE_NEEDS_X for ph in phases)
    T["x"] = nc.dram_tensor("x", [B_PER, S, D], F32, kind="ExternalInput")
    for name, (shape, dt) in WEIGHT_SPECS.items():
        T[name] = nc.dram_tensor(name, shape, dt, kind="ExternalInput")
    for name, (shape, dt) in SCRATCH_SPECS.items():
        kind = _scratch_kind(name, phases, keep)
        if kind is not None:
            T[name] = nc.dram_tensor(name, shape, dt, kind=kind)
    if any(ph in PHASE_WRITES_OUT for ph in phases):
        T["y"] = nc.dram_tensor("y", [B_PER, S, D], F32, kind="ExternalOutput")
    p = Prog(nc, stack)
    for ph in phases:
        EMITTERS[ph](p, T)
    p.finish()
    return nc, stack, T


def _ext_names(nc):
    ext_in, ext_out = [], []
    skip = {getattr(nc.partition_id_tensor, "name", None), "dbg_addr"}
    for alloc in nc.m.functions[0].allocations:
        if not isinstance(alloc, mybir.MemoryLocationSet):
            continue
        nm = alloc.memorylocations[0].name
        if nm in skip:
            continue
        if alloc.kind == "ExternalInput":
            ext_in.append(nm)
        elif alloc.kind == "ExternalOutput":
            ext_out.append(nm)
    return ext_in, ext_out


def run_groups(inputs, groups, keep=()):
    prep = prep_inputs(inputs)
    x = np.asarray(inputs["L0_attn_x"])
    per_core_state = []
    for c in range(NCORE):
        st = dict(prep)
        st["x"] = np.ascontiguousarray(x[c * B_PER:(c + 1) * B_PER])
        per_core_state.append(st)
    for group in groups:
        nc, stack, T = build_launch(group, keep=keep)
        ext_in, ext_out = _ext_names(nc)
        in_maps = [{n: per_core_state[c][n] for n in ext_in} for c in range(NCORE)]
        res = run_bass_kernel_spmd(nc, in_maps, core_ids=list(range(NCORE)))
        for c in range(NCORE):
            for n in ext_out:
                per_core_state[c][n] = res.results[c][n]
        stack.close()
    return per_core_state


def kernel(**inputs):
    state = run_groups(inputs, GROUPS)
    ys = [state[c]["y"] for c in range(NCORE)]
    return np.concatenate(ys, axis=0).astype(np.float32)
